# version 03
# baseline (speedup 1.0000x reference)
import contextlib
import numpy as np

import concourse.bass as bass
import concourse.mybir as mybir
from concourse.bass_utils import run_bass_kernel_spmd

F32 = mybir.dt.float32
BF16 = mybir.dt.bfloat16
F16 = mybir.dt.float16
U8 = mybir.dt.uint8
I16 = mybir.dt.int16
I32 = mybir.dt.int32
AF = mybir.ActivationFunctionType
ALU = mybir.AluOpType
AX = mybir.AxisListType

NSEQ = 2
T = 2048
NTOK = NSEQ * T
D = 1024
NCH = 8
H = 16
LN_EPS = 1e-5
RMS_EPS = 1.1920929e-07
Q_EPS = 1e-8

N_SUB = 4
PARANOID = 0
SAFE_SAME = True
DEBUG_H0 = False
ENGS = ("tensor", "vector", "scalar", "gpsimd", "sync")
CHANS = ENGS + ("qx", "qw", "qp", "qr", "qs", "qv", "qc")


class Buf:
    __slots__ = ("w", "r")

    def __init__(self):
        self.w = None
        self.r = []


class Sched:
    def __init__(self, nc, st):
        self.nc = nc
        self.sem = {c: st.enter_context(nc.semaphore("sem_" + c)) for c in CHANS}
        self.cnt = dict.fromkeys(CHANS, 0)
        self.steps = {e: [] for e in ENGS}
        self.waited = {e: dict.fromkeys(CHANS, 0) for e in ENGS}

    def op(self, eng, fn, reads=(), writes=(), dma=False, chan=None):
        if chan is None:
            chan = eng
        deps = []
        for b in reads:
            if b.w is not None:
                deps.append(b.w)
        for b in writes:
            if b.w is not None:
                deps.append(b.w)
            deps.extend(b.r)
        waits = {}
        for (pc, v, pdma) in deps:
            if pc == eng and not pdma and not SAFE_SAME:
                continue
            if self.waited[eng][pc] >= v:
                continue
            if waits.get(pc, 0) < v:
                waits[pc] = v
        for pc, v in waits.items():
            self.waited[eng][pc] = v
        inc = 16 if dma else 1
        self.cnt[chan] += inc
        self.steps[eng].append((tuple(waits.items()), fn, inc, chan))
        tok = (chan, self.cnt[chan], dma)
        for b in reads:
            b.r = [t for t in b.r if t[0] != chan] + [tok]
        for b in writes:
            b.w = tok
            b.r = []
        return tok

    def barrier(self):
        snap = [(c, self.cnt[c], True) for c in CHANS]
        fakes = []
        for t in snap:
            fb = Buf()
            fb.w = t
            fakes.append(fb)
        for e in ENGS:
            self.op(e, lambda en: en.nop(), reads=fakes)

    def finalize(self, block):
        for e in ENGS:
            items = self.steps[e]
            CH = 1200
            for i0 in range(0, len(items), CH):
                chunk = items[i0:i0 + CH]

                def body(en, chunk=chunk, e=e):
                    for waits, fn, inc, chan in chunk:
                        for pc, v in waits:
                            en.wait_ge(self.sem[pc], v)
                        fn(en).then_inc(self.sem[chan], inc)

                getattr(block, e)(body)


def build_nc(n_sub=None, _unused=None):
    if n_sub is None:
        n_sub = N_SUB
    nc = bass.Bass("TRN2", target_bir_lowering=False)
    dt = nc.dram_tensor

    x_in = dt("x_in", [NSEQ, D, T], F32, kind="ExternalInput").ap()
    ident = dt("ident", [128, 128], F32, kind="ExternalInput").ap()
    wdq = dt("wdq", [512, 64], F32, kind="ExternalInput").ap()
    wuqt = dt("wuqt", [512, 64], F32, kind="ExternalInput").ap()
    wdkv = dt("wdkv", [1024, 64], F32, kind="ExternalInput").ap()
    wukvt = dt("wukvt", [1024, 128], F32, kind="ExternalInput").ap()
    wqrt = dt("wqrt", [64, 64], F32, kind="ExternalInput").ap()
    wkrt = dt("wkrt", [64, 64], F32, kind="ExternalInput").ap()
    wot = dt("wot", [64, 64], F32, kind="ExternalInput").ap()
    qnw = dt("qnw", [64, 1], F32, kind="ExternalInput").ap()
    knw = dt("knw", [64, 1], F32, kind="ExternalInput").ap()
    lnsc = dt("lnsc", [128, 8], F32, kind="ExternalInput").ap()
    lnbi = dt("lnbi", [128, 8], F32, kind="ExternalInput").ap()
    bprojc = dt("bprojc", [128, 8], F32, kind="ExternalInput").ap()
    wqkv = dt("wqkv", [1024, 3072], F32, kind="ExternalInput").ap()
    wproj = dt("wproj", [1024, 1024], F32, kind="ExternalInput").ap()
    w1 = dt("w1", [4096, 1024], F32, kind="ExternalInput").ap()
    w1t = dt("w1t", [1024, 4096], F32, kind="ExternalInput").ap()
    b1c = dt("b1c", [128, 32], F32, kind="ExternalInput").ap()
    w2 = dt("w2", [1024, 4096], F32, kind="ExternalInput").ap()
    w2t = dt("w2t", [4096, 1024], F32, kind="ExternalInput").ap()
    b2c = dt("b2c", [128, 8], F32, kind="ExternalInput").ap()

    y = dt("y", [NSEQ, D, T], F32, kind="ExternalOutput").ap()
    if DEBUG_H0:
        dqop = dt("dqop", [128, 2048], F32, kind="ExternalOutput").ap()
        dkop = dt("dkop", [128, 2048], F32, kind="ExternalOutput").ap()
        dbias = dt("dbias", [128, 1], F32, kind="ExternalOutput").ap()
        dsc = dt("dsc", [1, 16], F32, kind="ExternalOutput").ap()
        dat = dt("dat", [128, 1024], F32, kind="ExternalOutput").ap()
        drsum = dt("drsum", [64, 2048], F32, kind="ExternalOutput").ap()
        dhsp = dt("dhsp", [8, 32, 128, 512], F16, kind="ExternalOutput").ap()
        dbr = dt("dbr", [8, 8, 128, 512], F32, kind="ExternalOutput").ap()
        drh = dt("drh", [8, 32, 128, 512], BF16, kind="ExternalOutput").ap()

    qkvb = dt("qkvb", [3, 8, 1024, 128], BF16, kind="Internal").ap()
    wprojb = dt("wprojb", [8, 128, 1024], BF16, kind="Internal").ap()
    wint1t = dt("wint1t", [32, 128, 1024], BF16, kind="Internal").ap()
    wint2t = dt("wint2t", [32, 128, 1024], BF16, kind="Internal").ap()
    hsp = dt("hsp", [8, 32, 128, 512], F16, kind="Internal").ap()
    xsp = dt("xsp", [NCH, 128, NTOK], F32, kind="Internal").ap()
    ccin = [dt(f"ccin{i}", [1, 1], F32, kind="Internal").ap() for i in range(4)]
    ccout = [dt(f"ccout{i}", [1, 1], F32, kind="Internal").ap() for i in range(4)]

    with contextlib.ExitStack() as st:
        s = Sched(nc, st)

        def sbt(stk, name, shape, dtype):
            t = stk.enter_context(nc.sbuf_tensor(name, shape, dtype))
            return t, Buf()

        def pst(stk, name, shape, dtype=F32):
            t = stk.enter_context(nc.psum_tensor(name, shape, dtype))
            return t, Buf()

        def V(fn, r=(), w=()):
            return s.op("vector", fn, r, w)

        def SS(fn, r=(), w=()):
            return s.op("scalar", fn, r, w)

        def TT(fn, r=(), w=()):
            return s.op("tensor", fn, r, w)

        def G(fn, r=(), w=()):
            return s.op("gpsimd", fn, r, w)

        def Q(fn, r=(), w=()):
            return s.op("sync", fn, r, w, dma=True, chan="qx")

        def QL(fn, r=(), w=()):
            return s.op("sync", fn, r, w, dma=True, chan="qw")

        def QP(fn, r=(), w=()):
            return s.op("sync", fn, r, w, dma=True, chan="qp")

        def QV(fn, r=(), w=()):
            return s.op("sync", fn, r, w, dma=True, chan="qv")

        def GD(fn, r=(), w=()):
            return s.op("gpsimd", fn, r, w, dma=True)

        IDF, bIDF = sbt(st, "IDF", [128, 128], F32)
        ONESR_F, bONESR_F = sbt(st, "ONESR_F", [1, 128], F32)
        ONESC_H, bONESC_H = sbt(st, "ONESC_H", [128, 1], F16)
        ONESC_B, bONESC_B = sbt(st, "ONESC_B", [128, 1], BF16)
        ZEROV, bZEROV = sbt(st, "ZEROV", [128, 1], F32)
        MASKB, bMASKB = sbt(st, "MASKB", [128, 128], BF16)
        MQT, bMQT = sbt(st, "MQT", [128, 64], BF16)
        NTT, bNTT = sbt(st, "NTT", [128, 128], BF16)
        WOVT, bWOVT = sbt(st, "WOVT", [128, 64], BF16)
        WQRP, bWQRP = sbt(st, "WQRP", [64, 64], BF16)
        WKRTB, bWKRTB = sbt(st, "WKRTB", [128, 64], BF16)
        WQK, bWQK = sbt(st, "WQK", [64, 1], F32)
        LNS, bLNS = sbt(st, "LNS", [128, 8], F32)
        LNB, bLNB = sbt(st, "LNB", [128, 8], F32)
        BPJ, bBPJ = sbt(st, "BPJ", [128, 8], F32)
        SW1, bSW1 = sbt(st, "SW1", [128, 32], F32)
        SW2, bSW2 = sbt(st, "SW2", [128, 8], F32)
        B1S, bB1S = sbt(st, "B1S", [128, 32], F32)
        B2S, bB2S = sbt(st, "B2S", [128, 8], F32)
        SV1, bSV1 = sbt(st, "SV1", [128, 32], F32)
        BQ1, bBQ1 = sbt(st, "BQ1", [128, 32], F32)
        SV2, bSV2 = sbt(st, "SV2", [128, 8], F32)
        BQ2, bBQ2 = sbt(st, "BQ2", [128, 8], F32)
        SCI1, bSCI1 = sbt(st, "SCI1", [128, 32], I32)
        RMU = [sbt(st, f"RMU{i}", [128, 512], F32) for i in range(2)]
        RRH = [sbt(st, f"RRH{i}", [128, 512], F32) for i in range(2)]
        RVR = [sbt(st, f"RVR{i}", [128, 512], F32) for i in range(2)]
        RR2 = [sbt(st, f"RR2{i}", [128, 512], F32) for i in range(2)]
        RSC = [sbt(st, f"RSC{i}", [128, 512], F32) for i in range(2)]
        SC11, bSC11 = sbt(st, "SC11", [1, 16], F32)
        BIASV, bBIASV = sbt(st, "BIASV", [128, 1], F32)
        S1V, bS1V = sbt(st, "S1V", [128, 2], F32)
        VTOK, bVTOK = sbt(st, "VTOK", [128, 16 * 80], BF16)
        ROWP, bROWP = sbt(st, "ROWP", [1, 128], F32)
        RROW512, bRROW512 = sbt(st, "RROW512", [1, 512], F32)
        SCW, bSCW = sbt(st, "SCW", [1, 128], F32)
        ONES4, bONES4 = sbt(st, "ONES4", [128, 128], F32)
        RROW0, bRROW0 = sbt(st, "RROW0", [1, 128], F32)

        try:
            nc.const_aps.register(F32, 0.0, ZEROV[:, 0:1])
        except Exception:
            pass

        def rsqrt_inplace(ap, buf, scale=-0.5):
            SS(lambda e, a=ap: e.activation(a, a, AF.Ln), r=[buf], w=[buf])
            SS(lambda e, a=ap, sc=scale: e.activation(a, a, AF.Exp, scale=sc), r=[buf], w=[buf])

        def srow(RT, j):
            t, _ = RT[j // 4]
            return t[32 * (j % 4):32 * (j % 4) + 1, :]

        def sbuf_of(RT, j):
            return RT[j // 4][1]

        def sview(RT, i):
            return RT[i][0][0:97, :], RT[i][1]

        def slab_op2(RTd, RTa, fn2, nhalf=2):
            for i in range(nhalf):
                dv, db = sview(RTd, i)
                av, ab = sview(RTa, i)
                V(lambda e, dv=dv, av=av: fn2(e, dv, av), r=[ab], w=[db])

        def cross_part_max(src_ap, src_buf, p_t, p_buf, dst_col):
            TT(lambda e: e.transpose(p_t[0:1, 0:128], src_ap, IDF[:]),
               r=[src_buf, bIDF], w=[p_buf])
            V(lambda e: e.tensor_copy(ROWP[0:1, 0:128], p_t[0:1, 0:128]), r=[p_buf], w=[bROWP])
            V(lambda e, dc_=dst_col: e.reduce_max(SC11[0:1, dc_:dc_ + 1], ROWP[0:1, 0:128],
                                                  axis=AX.X), r=[bROWP], w=[bSC11])

        def replicate(RT, j0, nblks, dst_ap, dst_buf, npart, PR, bPR, prw, scale=1.0):
            for jj in range(nblks):
                j = j0 + jj
                off = 512 * jj
                po = off % prw
                bp = 32 * (j % 4)
                if bp == 96:
                    V(lambda e, j=j: e.tensor_copy(RROW512[0:1, :], srow(RT, j)),
                      r=[sbuf_of(RT, j)], w=[bRROW512])
                    TT(lambda e, po=po: e.matmul(
                        PR[0:npart, po:po + 512], ONESR_F[0:1, 0:npart],
                        RROW512[0:1, :], start=True, stop=True),
                       r=[bONESR_F, bRROW512], w=[bPR])
                else:
                    TT(lambda e, j=j, po=po, bp=bp: e.matmul(
                        PR[0:npart, po:po + 512], ONES4[bp:bp + 1, 0:npart],
                        srow(RT, j), start=True, stop=True),
                       r=[bONES4, sbuf_of(RT, j)], w=[bPR])
                V(lambda e, off=off, po=po, sc=scale: e.tensor_scalar(
                    dst_ap[0:npart, off:off + 512], PR[0:npart, po:po + 512], float(sc), None, ALU.mult),
                  r=[bPR], w=[dst_buf])

        with contextlib.ExitStack() as p0:
            PSA, bPSA = pst(p0, "PSA0", [128, 512])
            PSB, bPSB = pst(p0, "PSB0", [128, 512])
            PSN, bPSN = pst(p0, "PSN0", [128, 512])

            QL(lambda e: e.dma_start(out=IDF[:], in_=ident), w=[bIDF])
            V(lambda e: e.memset(ONESR_F[:], 1.0), w=[bONESR_F])
            V(lambda e: e.memset(ONES4[:], 1.0), w=[bONES4])
            V(lambda e: e.memset(ONESC_H[:], 1.0), w=[bONESC_H])
            V(lambda e: e.memset(ONESC_B[:], 1.0), w=[bONESC_B])
            V(lambda e: e.memset(ZEROV[:], 0.0), w=[bZEROV])
            V(lambda e: e.memset(VTOK[:], 1.0), w=[bVTOK])
            V(lambda e: e.memset(MASKB[:], 1.0), w=[bMASKB])
            G(lambda e: e.affine_select(MASKB[:], MASKB[:], [[1, 128]], ALU.is_ge,
                                        0.0, base=0, channel_multiplier=-1),
              r=[bMASKB], w=[bMASKB])
            for src, dst, bdst in ((lnsc, LNS, bLNS), (lnbi, LNB, bLNB), (bprojc, BPJ, bBPJ),
                                   (b1c, B1S, bB1S), (b2c, B2S, bB2S)):
                QL(lambda e, d=dst, sp=src: e.dma_start(out=d[:], in_=sp), w=[bdst])

            SWDQ, bSWDQ = sbt(p0, "SWDQ", [128, 256], BF16)
            SWUQT, bSWUQT = sbt(p0, "SWUQT", [128, 256], BF16)
            SWDKV, bSWDKV = sbt(p0, "SWDKV", [128, 512], BF16)
            SWUKVT, bSWUKVT = sbt(p0, "SWUKVT", [128, 1024], BF16)
            NFUL, bNFUL = sbt(p0, "NFUL", [128, 64], BF16)
            SWOT, bSWOT = sbt(p0, "SWOT", [64, 64], BF16)
            SMALLF, bSMALLF = sbt(p0, "SMALLF", [128, 1024], F32)
            QNW, bQNW = sbt(p0, "QNW", [64, 1], F32)
            KNW, bKNW = sbt(p0, "KNW", [64, 1], F32)

            def load_chunked(dst, bdst, src, nch_, width):
                for j in range(nch_):
                    QL(lambda e, j=j, src=src, width=width: e.dma_start(
                        out=SMALLF[:, 0:width], in_=src[128 * j:128 * (j + 1), :]),
                      r=[bSMALLF], w=[bSMALLF])
                    V(lambda e, j=j, dst=dst, width=width: e.tensor_copy(
                        dst[:, width * j:width * (j + 1)], SMALLF[:, 0:width]),
                      r=[bSMALLF], w=[bdst])

            load_chunked(SWDQ, bSWDQ, wdq, 4, 64)
            load_chunked(SWUQT, bSWUQT, wuqt, 4, 64)
            load_chunked(SWDKV, bSWDKV, wdkv, 8, 64)
            load_chunked(SWUKVT, bSWUKVT, wukvt, 8, 128)
            QL(lambda e: e.dma_start(out=QNW[:], in_=qnw), w=[bQNW])
            QL(lambda e: e.dma_start(out=KNW[:], in_=knw), w=[bKNW])
            V(lambda e: e.tensor_tensor(WQK[:], QNW[:], KNW[:], ALU.mult), r=[bQNW, bKNW], w=[bWQK])
            QL(lambda e: e.dma_start(out=SMALLF[0:64, 0:64], in_=wot), r=[bSMALLF], w=[bSMALLF])
            V(lambda e: e.tensor_copy(SWOT[:], SMALLF[0:64, 0:64]), r=[bSMALLF], w=[bSWOT])

            for j in range(4):
                TT(lambda e, j=j: e.matmul(PSA[0:64, 0:64], SWDQ[:, 64 * j:64 * (j + 1)],
                                           SWUQT[:, 64 * j:64 * (j + 1)], start=(j == 0), stop=(j == 3)),
                   r=[bSWDQ, bSWUQT], w=[bPSA])
            V(lambda e: e.tensor_copy(MQT[0:64, :], PSA[0:64, 0:64]), r=[bPSA], w=[bMQT])
            V(lambda e: e.tensor_copy(MQT[64:128, :], MQT[0:64, :]), r=[bMQT], w=[bMQT])
            for j in range(8):
                TT(lambda e, j=j: e.matmul(PSB[0:128, 0:64], SWUKVT[:, 128 * j:128 * (j + 1)],
                                           SWDKV[:, 64 * j:64 * (j + 1)], start=(j == 0), stop=(j == 7)),
                   r=[bSWUKVT, bSWDKV], w=[bPSB])
            V(lambda e: e.tensor_copy(NFUL[:], PSB[0:128, 0:64]), r=[bPSB], w=[bNFUL])
            for j in range(8):
                TT(lambda e, j=j: e.matmul(PSA[0:64, 0:128], SWDKV[:, 64 * j:64 * (j + 1)],
                                           SWUKVT[:, 128 * j:128 * (j + 1)], start=(j == 0), stop=(j == 7)),
                   r=[bSWUKVT, bSWDKV], w=[bPSA])
            V(lambda e: e.tensor_copy(NTT[0:64, :], PSA[0:64, 0:128]), r=[bPSA], w=[bNTT])
            V(lambda e: e.tensor_copy(NTT[64:128, :], NTT[0:64, :]), r=[bNTT], w=[bNTT])
            V(lambda e: e.tensor_copy(NFUL[0:64, 0:64] if False else SWDQ[0:64, 0:64], NFUL[64:128, 0:64]),
              r=[bNFUL], w=[bSWDQ])
            TT(lambda e: e.matmul(PSN[0:64, 0:64], SWDQ[0:64, 0:64], SWOT[:], start=True, stop=True),
               r=[bSWDQ, bSWOT], w=[bPSN])
            V(lambda e: e.tensor_copy(WOVT[0:64, :], PSN[0:64, 0:64]), r=[bPSN], w=[bWOVT])
            V(lambda e: e.tensor_copy(WOVT[64:128, :], WOVT[0:64, :]), r=[bWOVT], w=[bWOVT])
            QL(lambda e: e.dma_start(out=SMALLF[0:64, 0:64], in_=wqrt), r=[bSMALLF], w=[bSMALLF])
            V(lambda e: e.tensor_scalar(WQRP[:], SMALLF[0:64, 0:64], QNW[:], None, ALU.mult),
              r=[bSMALLF, bQNW], w=[bWQRP])
            QL(lambda e: e.dma_start(out=SMALLF[0:64, 64:128], in_=wkrt), r=[bSMALLF], w=[bSMALLF])
            V(lambda e: e.tensor_copy(WKRTB[0:64, :], SMALLF[0:64, 64:128]), r=[bSMALLF], w=[bWKRTB])
            V(lambda e: e.tensor_copy(WKRTB[64:128, :], WKRTB[0:64, :]), r=[bWKRTB], w=[bWKRTB])

            WSTG, bWSTG = sbt(p0, "WSTG", [128, 1024], F32)
            WSTG2, bWSTG2 = sbt(p0, "WSTG2", [128, 1024], F32)
            WBF, bWBF = sbt(p0, "WBF", [128, 512], BF16)
            WI16, bWI16 = sbt(p0, "WI16", [128, 512], I16)
            RREC, bRREC = sbt(p0, "RREC", [128, 32], F32)
            REPG, bREPG = sbt(p0, "REPG", [128, 512], F32)
            MXW, bMXW = sbt(p0, "MXW", [128, 32], F32)

            for cb in range(6):
                for dc in range(8):
                    QL(lambda e, cb=cb, dc=dc: e.dma_start(
                        out=WSTG[:, 0:512], in_=wqkv[128 * dc:128 * (dc + 1), 512 * cb:512 * (cb + 1)]),
                      r=[bWSTG], w=[bWSTG])
                    sc = 0.125 if cb < 2 else 1.0
                    V(lambda e, sc=sc: e.tensor_scalar(WBF[:], WSTG[:, 0:512], float(sc), None, ALU.mult),
                      r=[bWSTG], w=[bWBF])
                    for jj in range(4):
                        col = cb * 512 + jj * 128
                        g, p = col // 1024, (col % 1024) // 128
                        QP(lambda e, g=g, p=p, dc=dc, jj=jj: e.dma_start(
                            out=qkvb[g, p, 128 * dc:128 * (dc + 1), :], in_=WBF[:, 128 * jj:128 * (jj + 1)]),
                          r=[bWBF], w=[bWBF])

            for dc in range(8):
                QL(lambda e, dc=dc: e.dma_start(out=WSTG[:, :], in_=wproj[128 * dc:128 * (dc + 1), :]),
                  r=[bWSTG], w=[bWSTG])
                for half in range(2):
                    V(lambda e, half=half: e.tensor_copy(WBF[:], WSTG[:, 512 * half:512 * (half + 1)]),
                      r=[bWSTG], w=[bWBF])
                    for jj in range(4):
                        oc = 4 * half + jj
                        QP(lambda e, oc=oc, jj=jj, dc=dc: e.dma_start(
                            out=wprojb[oc, :, 128 * dc:128 * (dc + 1)],
                            in_=WBF[:, 128 * jj:128 * (jj + 1)]),
                          r=[bWBF], w=[bWBF])
            for m in range(32):
                QL(lambda e, m=m: e.dma_start(out=WSTG2[:, :], in_=w1[128 * m:128 * (m + 1), :]),
                  r=[bWSTG2], w=[bWSTG2])
                V(lambda e, m=m: e.reduce_max(MXW[:, m:m + 1], WSTG2[:, :], axis=AX.X,
                                              apply_absolute_value=True), r=[bWSTG2], w=[bMXW])
            V(lambda e: e.tensor_scalar(SW1[:], MXW[:], float(Q_EPS), 1.0 / 127.0, ALU.max, ALU.mult),
              r=[bMXW], w=[bSW1])
            V(lambda e: e.reciprocal(RREC[:], SW1[:]), r=[bSW1], w=[bRREC])
            for g in range(8):
                for jj in range(4):
                    j = 4 * g + jj
                    TT(lambda e, j=j: e.transpose(PSA[0:1, 0:128], RREC[:, j:j + 1], IDF[:]),
                       r=[bRREC, bIDF], w=[bPSA])
                    V(lambda e: e.tensor_copy(RROW0[0:1, :], PSA[0:1, 0:128]),
                      r=[bPSA], w=[bRROW0])
                    TT(lambda e, jj=jj: e.matmul(PSB[:, 128 * jj:128 * (jj + 1)],
                                                 ONESR_F[0:1, :], RROW0[0:1, 0:128],
                                                 start=True, stop=True),
                       r=[bONESR_F, bRROW0], w=[bPSB])
                V(lambda e: e.tensor_copy(REPG[:], PSB[:, 0:512]), r=[bPSB], w=[bREPG])
                for dc in range(8):
                    QL(lambda e, g=g, dc=dc: e.dma_start(
                        out=WSTG[:, 0:512], in_=w1t[128 * dc:128 * (dc + 1), 512 * g:512 * (g + 1)]),
                      r=[bWSTG], w=[bWSTG])
                    V(lambda e: e.tensor_tensor(WI16[:], WSTG[:, 0:512], REPG[:], ALU.mult),
                      r=[bWSTG, bREPG], w=[bWI16])
                    V(lambda e: e.tensor_copy(WBF[:], WI16[:]), r=[bWI16], w=[bWBF])
                    for jj in range(4):
                        QP(lambda e, g=g, jj=jj, dc=dc: e.dma_start(
                            out=wint1t[4 * g + jj, :, 128 * dc:128 * (dc + 1)],
                            in_=WBF[:, 128 * jj:128 * (jj + 1)]),
                          r=[bWBF], w=[bWBF])

            for m in range(8):
                for qq in range(4):
                    QL(lambda e, m=m, qq=qq: e.dma_start(
                        out=WSTG2[:, :], in_=w2[128 * m:128 * (m + 1), 1024 * qq:1024 * (qq + 1)]),
                      r=[bWSTG2], w=[bWSTG2])
                    V(lambda e, qq=qq: e.reduce_max(MXW[:, qq:qq + 1], WSTG2[:, :], axis=AX.X,
                                                    apply_absolute_value=True), r=[bWSTG2], w=[bMXW])
                V(lambda e, m=m: e.reduce_max(MXW[:, 8 + m:9 + m], MXW[:, 0:4], axis=AX.X),
                  r=[bMXW], w=[bMXW])
            V(lambda e: e.tensor_scalar(SW2[:], MXW[:, 8:16], float(Q_EPS), 1.0 / 127.0, ALU.max, ALU.mult),
              r=[bMXW], w=[bSW2])
            V(lambda e: e.reciprocal(RREC[:, 0:8], SW2[:]), r=[bSW2], w=[bRREC])
            for g in range(2):
                for jj in range(4):
                    j = 4 * g + jj
                    TT(lambda e, j=j: e.transpose(PSA[0:1, 0:128], RREC[:, j:j + 1], IDF[:]),
                       r=[bRREC, bIDF], w=[bPSA])
                    V(lambda e: e.tensor_copy(RROW0[0:1, :], PSA[0:1, 0:128]),
                      r=[bPSA], w=[bRROW0])
                    TT(lambda e, jj=jj: e.matmul(PSB[:, 128 * jj:128 * (jj + 1)],
                                                 ONESR_F[0:1, :], RROW0[0:1, 0:128],
                                                 start=True, stop=True),
                       r=[bONESR_F, bRROW0], w=[bPSB])
                V(lambda e: e.tensor_copy(REPG[:], PSB[:, 0:512]), r=[bPSB], w=[bREPG])
                for fc in range(32):
                    QL(lambda e, g=g, fc=fc: e.dma_start(
                        out=WSTG[:, 0:512], in_=w2t[128 * fc:128 * (fc + 1), 512 * g:512 * (g + 1)]),
                      r=[bWSTG], w=[bWSTG])
                    V(lambda e: e.tensor_tensor(WI16[:], WSTG[:, 0:512], REPG[:], ALU.mult),
                      r=[bWSTG, bREPG], w=[bWI16])
                    V(lambda e: e.tensor_copy(WBF[:], WI16[:]), r=[bWI16], w=[bWBF])
                    for jj in range(4):
                        QP(lambda e, g=g, jj=jj, fc=fc: e.dma_start(
                            out=wint2t[fc, :, 128 * (4 * g + jj):128 * (4 * g + jj + 1)],
                            in_=WBF[:, 128 * jj:128 * (jj + 1)]),
                          r=[bWBF], w=[bWBF])
            s.barrier()
        s.barrier()
        print("after P0: sbuf bytes left:", nc.sbuf_bytes_remaining)

        def ln_stats(X, bX, PRx, bPRx, PRy, bPRy, XBT, bXBT, XSQ, bXSQ, ntok):
            nblk = ntok // 512
            for i in range((nblk + 3) // 4):
                for j4 in range(min(4, nblk - 4 * i)):
                    blk = 4 * i + j4
                    for c in range(NCH):
                        V(lambda e, c=c, blk=blk: e.tensor_copy(XBT[:, 0:512],
                                                                X[c][:, 512 * blk:512 * (blk + 1)]),
                          r=[bX[c], bXBT], w=[bXBT])
                        V(lambda e: e.tensor_tensor(XSQ[:, 0:512], XBT[:, 0:512], XBT[:, 0:512],
                                                    ALU.mult), r=[bXBT], w=[bXSQ])
                        TT(lambda e, c=c, j4=j4: e.matmul(PRx[32 * j4:32 * j4 + 1, 0:512],
                                                          ONESC_B[:], XBT[:, 0:512],
                                                          start=(c == 0), stop=(c == NCH - 1),
                                                          tile_position=(0, 32 * j4)),
                           r=[bONESC_B, bXBT], w=[bPRx])
                        TT(lambda e, c=c, j4=j4: e.matmul(PRy[32 * j4:32 * j4 + 1, 0:512],
                                                          ONESC_H[:], XSQ[:, 0:512],
                                                          start=(c == 0), stop=(c == NCH - 1),
                                                          tile_position=(0, 32 * j4)),
                           r=[bONESC_H, bXSQ], w=[bPRy])
                V(lambda e, i=i: e.tensor_scalar(RMU[i][0][0:97, :], PRx[0:97, 0:512],
                                                 1.0 / D, None, ALU.mult), r=[bPRx], w=[RMU[i][1]])
                V(lambda e, i=i: e.tensor_scalar(RVR[i][0][0:97, :], PRy[0:97, 0:512],
                                                 1.0 / D, None, ALU.mult), r=[bPRy], w=[RVR[i][1]])
            nh = (nblk + 3) // 4
            for i in range(nh):
                mv, mb = sview(RMU, i)
                vv, vb = sview(RVR, i)
                sv_, sb_ = sview(RSC, i)
                rv, rb = sview(RRH, i)
                V(lambda e, sv_=sv_, mv=mv: e.tensor_tensor(sv_, mv, mv, ALU.mult), r=[mb], w=[sb_])
                V(lambda e, vv=vv, sv_=sv_: e.tensor_tensor(vv, vv, sv_, ALU.subtract),
                  r=[vb, sb_], w=[vb])
                V(lambda e, rv=rv, vv=vv: e.tensor_scalar(rv, vv, float(LN_EPS), None, ALU.add),
                  r=[vb], w=[rb])
                SS(lambda e, rv=rv: e.activation(rv, rv, AF.Ln), r=[rb], w=[rb])
                SS(lambda e, rv=rv: e.activation(rv, rv, AF.Exp, scale=-0.5), r=[rb], w=[rb])

        def emit_attn_a(X, bX):
            with contextlib.ExitStack() as pp:
                PS1, bPS1 = pst(pp, "PS1a", [128, 1024])
                PS2, bPS2 = pst(pp, "PS2a", [128, 1024])
                PC, bPC = pst(pp, "PCa", [65, 2048])
                XBT, bXBT = sbt(pp, "XBTa", [128, 512], BF16)
                XSQ, bXSQ = sbt(pp, "XSQa", [128, 512], F16)
                MUB, bMUB = sbt(pp, "MUBa", [128, 2048], BF16)
                RHOB, bRHOB = sbt(pp, "RHOBa", [128, 2048], BF16)
                SLICE, bSLICE = sbt(pp, "SLICEa", [128, 2048], BF16)
                QOP, bQOP = sbt(pp, "QOPa", [128, 2048], BF16)
                KOP, bKOP = sbt(pp, "KOPa", [128, 2048], BF16)
                AT0, bAT0 = sbt(pp, "AT0a", [128, 1024], BF16)
                AT1, bAT1 = sbt(pp, "AT1a", [128, 1024], BF16)
                CB, bCB = sbt(pp, "CBa", [64, 2048], BF16)
                RQK, bRQK = sbt(pp, "RQKa", [64, 2048], BF16)
                RSUM, bRSUM = sbt(pp, "RSUMa", [64, 1024], F32)
                XUPD, bXUPD = sbt(pp, "XUPDa", [128, 2048], F32)
                SC0, bSC0 = sbt(pp, "SC0a", [128, 512], F32)
                VFM, bVFM = CB, bCB
                print("attn_a: sbuf left:", nc.sbuf_bytes_remaining)

                ln_stats(X, bX, PS1, bPS1, PS2, bPS2, XBT, bXBT, XSQ, bXSQ, NTOK)

                def fm64(lhs_t, lhs_buf, wcols, dst_ap, dst_buf, r0, PSx, bPSx):
                    for sg in range(4):
                        o = 512 * (sg % 2)
                        TT(lambda e, sg=sg, o=o, r0=r0: e.matmul(
                            PSx[0:64, o:o + 512], lhs_t[r0:r0 + 64, wcols[0]:wcols[1]],
                            SLICE[r0:r0 + 64, 512 * sg:512 * (sg + 1)],
                            start=True, stop=True),
                           r=[lhs_buf, bSLICE], w=[bPSx])
                        V(lambda e, sg=sg, o=o, dst_ap=dst_ap: e.tensor_copy(
                            dst_ap[:, 512 * sg:512 * (sg + 1)], PSx[0:64, o:o + 512]),
                          r=[bPSx], w=[dst_buf])

                def fm64_cb(lhs_ap, lhs_buf, dst_lo, dst_ap, dst_buf, PSx, bPSx, mul_rqk=False):
                    for sg in range(4):
                        o = 512 * (sg % 2)
                        TT(lambda e, sg=sg, o=o, lhs_ap=lhs_ap: e.matmul(
                            PSx[0:64, o:o + 512], lhs_ap, CB[:, 512 * sg:512 * (sg + 1)],
                            start=True, stop=True),
                           r=[lhs_buf, bCB], w=[bPSx])
                        if mul_rqk:
                            V(lambda e, sg=sg, o=o: e.tensor_tensor(
                                dst_ap[dst_lo:dst_lo + 64, 512 * sg:512 * (sg + 1)],
                                PSx[0:64, o:o + 512], RQK[:, 512 * sg:512 * (sg + 1)], ALU.mult),
                              r=[bPSx, bRQK], w=[dst_buf])
                        else:
                            V(lambda e, sg=sg, o=o: e.tensor_copy(
                                dst_ap[dst_lo:dst_lo + 64, 512 * sg:512 * (sg + 1)],
                                PSx[0:64, o:o + 512]),
                              r=[bPSx], w=[dst_buf])

                def rms_row(scale):
                    for sg in range(4):
                        V(lambda e, sg=sg: e.tensor_tensor(XSQ[0:64, 0:512],
                                                           CB[:, 512 * sg:512 * (sg + 1)],
                                                           CB[:, 512 * sg:512 * (sg + 1)], ALU.mult),
                          r=[bCB, bXSQ], w=[bXSQ])
                        TT(lambda e, sg=sg: e.matmul(PS1[32 * sg:32 * sg + 1, 0:512],
                                                     ONESC_H[0:64, :], XSQ[0:64, 0:512],
                                                     start=True, stop=True,
                                                     tile_position=(0, 32 * sg)),
                           r=[bONESC_H, bXSQ], w=[bPS1])
                    V(lambda e: e.tensor_scalar(RSC[0][0][0:97, :], PS1[0:97, 0:512],
                                                1.0 / 64.0, float(RMS_EPS), ALU.mult, ALU.add),
                      r=[bPS1], w=[RSC[0][1]])
                    rv, rb = sview(RSC, 0)
                    SS(lambda e, rv=rv: e.activation(rv, rv, AF.Ln), r=[rb], w=[rb])
                    SS(lambda e, rv=rv: e.activation(rv, rv, AF.Exp, scale=-0.5), r=[rb], w=[rb])
                    replicate(RSC, 0, 4, RQK, bRQK, 64, PS1, bPS1, 1024, scale=scale)

                def colmax_sq(OP, bOP, cbase):
                    for sg in range(4):
                        o = 512 * (sg % 2)
                        V(lambda e, sg=sg, OP=OP: e.tensor_tensor(
                            XSQ[:, 0:512], OP[:, 512 * sg:512 * (sg + 1)],
                            OP[:, 512 * sg:512 * (sg + 1)], ALU.mult),
                          r=[bOP, bXSQ], w=[bXSQ])
                        TT(lambda e, o=o: e.matmul(PS1[0:1, o:o + 512], ONESC_H[:], XSQ[:, 0:512],
                                                   start=True, stop=True),
                           r=[bONESC_H, bXSQ], w=[bPS1])
                        V(lambda e, sg=sg, o=o, cbase=cbase: e.reduce_max(
                            SC11[0:1, cbase + sg:cbase + sg + 1], PS1[0:1, o:o + 512], axis=AX.X),
                          r=[bPS1], w=[bSC11])

                for b in range(NSEQ):
                    t0 = b * T
                    replicate(RMU, 4 * b, 4, MUB, bMUB, 128, PS1, bPS1, 1024)
                    replicate(RRH, 4 * b, 4, RHOB, bRHOB, 128, PS1, bPS1, 1024)
                    for c in range(NCH):
                        for sg in range(4):
                            V(lambda e, c=c, sg=sg, t0=t0: e.tensor_tensor(
                                SC0[:], X[c][:, t0 + 512 * sg:t0 + 512 * (sg + 1)],
                                MUB[:, 512 * sg:512 * (sg + 1)], ALU.subtract),
                              r=[bX[c], bMUB], w=[bSC0])
                            V(lambda e, sg=sg: e.tensor_tensor(
                                SLICE[:, 512 * sg:512 * (sg + 1)], SC0[:],
                                RHOB[:, 512 * sg:512 * (sg + 1)], ALU.mult),
                              r=[bSC0, bRHOB], w=[bSLICE])
                        for hh in range(2):
                            r0 = 64 * hh
                            fm64(WOVT, bWOVT, (0, 64), VFM, bVFM, r0, PS1, bPS1)
                            for tt_ in range(16):
                                QV(lambda e, tt_=tt_: e.dma_start_transpose(
                                    VTOK[:, 80 * tt_:80 * tt_ + 64], VFM[:, 128 * tt_:128 * (tt_ + 1)]),
                                  r=[bVFM, bVTOK], w=[bVTOK])
                            fm64(NTT, bNTT, (0, 64), CB, bCB, r0, PS2, bPS2)
                            rms_row(1.0)
                            V(lambda e: e.tensor_tensor(KOP[0:64, :], CB[:], RQK[:], ALU.mult),
                              r=[bCB, bRQK], w=[bKOP])
                            for sg in range(4):
                                o = 512 * (sg % 2)
                                TT(lambda e, sg=sg, o=o, r0=r0: e.matmul(
                                    PS2[0:64, o:o + 512], WKRTB[r0:r0 + 64, :],
                                    SLICE[r0:r0 + 64, 512 * sg:512 * (sg + 1)],
                                    start=True, stop=True),
                                   r=[bWKRTB, bSLICE], w=[bPS2])
                                V(lambda e, sg=sg, o=o: e.tensor_copy(
                                    KOP[64:128, 512 * sg:512 * (sg + 1)], PS2[0:64, o:o + 512]),
                                  r=[bPS2], w=[bKOP])
                            fm64(MQT, bMQT, (0, 64), CB, bCB, r0, PS2, bPS2)
                            rms_row(0.125)
                            V(lambda e: e.scalar_tensor_tensor(QOP[0:64, :], CB[:], WQK[:], RQK[:],
                                                               ALU.mult, ALU.mult),
                              r=[bCB, bWQK, bRQK], w=[bQOP])
                            fm64_cb(WQRP[:], bWQRP, 64, QOP, bQOP, PS2, bPS2, mul_rqk=True)
                            colmax_sq(QOP, bQOP, 0)
                            colmax_sq(KOP, bKOP, 4)
                            V(lambda e: e.reduce_max(SC11[0:1, 8:9], SC11[0:1, 0:4], axis=AX.X),
                              r=[bSC11], w=[bSC11])
                            V(lambda e: e.reduce_max(SC11[0:1, 9:10], SC11[0:1, 4:8], axis=AX.X),
                              r=[bSC11], w=[bSC11])
                            V(lambda e: e.tensor_tensor(SC11[0:1, 10:11], SC11[0:1, 8:9],
                                                        SC11[0:1, 9:10], ALU.mult),
                              r=[bSC11], w=[bSC11])
                            V(lambda e: e.tensor_scalar(SCW[0:1, :], ONESR_F[0:1, :],
                                                        SC11[0:1, 10:11], None, ALU.mult),
                              r=[bONESR_F, bSC11], w=[bSCW])
                            SS(lambda e: e.activation(SCW[0:1, :], SCW[0:1, :], AF.Ln),
                               r=[bSCW], w=[bSCW])
                            SS(lambda e: e.activation(SCW[0:1, :], SCW[0:1, :], AF.Exp,
                                                      scale=0.5), r=[bSCW], w=[bSCW])
                            TT(lambda e: e.matmul(PS1[:, 0:1], SCW[0:1, :], ONESR_F[0:1, 0:1],
                                                  start=True, stop=True),
                               r=[bONESR_F, bSCW], w=[bPS1])
                            V(lambda e: e.tensor_scalar(BIASV[:], PS1[:, 0:1], -1.0, None, ALU.mult),
                              r=[bPS1], w=[bBIASV])
                            if PARANOID >= 1:
                                s.barrier()
                            if DEBUG_H0 and b == 0 and c == 0 and hh == 0:
                                V(lambda e: e.tensor_copy(XUPD[:, 0:2048], QOP[:]), r=[bQOP], w=[bXUPD])
                                Q(lambda e: e.dma_start(out=dqop, in_=XUPD[:, 0:2048]), r=[bXUPD], w=[bXUPD])
                                V(lambda e: e.tensor_copy(XUPD[:, 0:2048], KOP[:]), r=[bKOP], w=[bXUPD])
                                Q(lambda e: e.dma_start(out=dkop, in_=XUPD[:, 0:2048]), r=[bXUPD], w=[bXUPD])
                                Q(lambda e: e.dma_start(out=dbias, in_=BIASV[:]), r=[bBIASV], w=[bBIASV])
                                Q(lambda e: e.dma_start(out=dsc, in_=SC11[:]), r=[bSC11], w=[bSC11])
                            for kb in range(16):
                                qlo = 128 * kb
                                qa0 = 512 * (qlo // 512)
                                qq = qa0
                                while qq < T:
                                    qhi = min(((qq // 1024) + 1) * 1024, T)
                                    PSs, bPSs, ATs, bATs = (PS1, bPS1, AT0, bAT0) if (qq // 1024) % 2 == 0 \
                                        else (PS2, bPS2, AT1, bAT1)
                                    o0 = qq % 1024
                                    for pz in range(qq, qhi, 512):
                                        TT(lambda e, kb=kb, pz=pz, PSs=PSs: e.matmul(
                                            PSs[:, pz % 1024:pz % 1024 + 512],
                                            KOP[:, 128 * kb:128 * (kb + 1)], QOP[:, pz:pz + 512],
                                            start=True, stop=True),
                                           r=[bKOP, bQOP], w=[bPSs])
                                    SS(lambda e, PSs=PSs, ATs=ATs, o0=o0, n=qhi - qq: e.activation(
                                        ATs[:, o0:o0 + n], PSs[:, o0:o0 + n], AF.Exp,
                                        bias=BIASV[:, 0:1], scale=1.0),
                                       r=[bPSs, bBIASV], w=[bATs])
                                    if qq == qa0:
                                        if qlo > qa0:
                                            V(lambda e, ATs=ATs, o0=o0, nz=qlo - qa0: e.memset(
                                                ATs[:, o0:o0 + nz], 0.0), r=[bATs], w=[bATs])
                                        V(lambda e, ATs=ATs, od=qlo % 1024: e.tensor_tensor(
                                            ATs[:, od:od + 128], ATs[:, od:od + 128], MASKB[:],
                                            ALU.mult), r=[bMASKB, bATs], w=[bATs])
                                    for pz in range(qq, qhi, 512):
                                        TT(lambda e, kb=kb, pz=pz, ATs=ATs: e.matmul(
                                            PC[0:65, pz:pz + 512], VTOK[:, 80 * kb:80 * kb + 65],
                                            ATs[:, pz % 1024:pz % 1024 + 512],
                                            start=(kb == 0), stop=(kb == min(15, (pz + 511) // 128))),
                                           r=[bVTOK, bATs], w=[bPC])
                                    qq = qhi
                                if PARANOID >= 2:
                                    s.barrier()
                                if DEBUG_H0 and b == 0 and c == 0 and hh == 0 and kb == 0:
                                    V(lambda e: e.tensor_copy(XUPD[:, 0:1024], AT0[:]), r=[bAT0], w=[bXUPD])
                                    Q(lambda e: e.dma_start(out=dat, in_=XUPD[:, 0:1024]), r=[bXUPD], w=[bXUPD])
                            if PARANOID >= 1:
                                s.barrier()
                            for sg in range(4):
                                V(lambda e, sg=sg: e.reciprocal(srow(RSC, sg),
                                                                PC[64:65, 512 * sg:512 * (sg + 1)]),
                                  r=[bPC], w=[RSC[0][1]])
                            for sg in range(2):
                                replicate(RSC, 2 * sg, 2, RSUM, bRSUM, 64, PS1, bPS1, 1024)
                                V(lambda e, sg=sg, r0=r0: e.tensor_tensor(
                                    XUPD[r0:r0 + 64, 1024 * sg:1024 * (sg + 1)],
                                    PC[0:64, 1024 * sg:1024 * (sg + 1)],
                                    RSUM[:, 0:1024], ALU.mult),
                                  r=[bPC, bRSUM], w=[bXUPD])
                        for sg in range(4):
                            V(lambda e, c=c, sg=sg, t0=t0: e.tensor_tensor(
                                X[c][:, t0 + 512 * sg:t0 + 512 * (sg + 1)],
                                X[c][:, t0 + 512 * sg:t0 + 512 * (sg + 1)],
                                XUPD[:, 512 * sg:512 * (sg + 1)], ALU.add),
                              r=[bXUPD, bX[c]], w=[bX[c]])
            s.barrier()

        def emit_ffn(X, bX, cc_idx0):
            sfx = str(cc_idx0)
            with contextlib.ExitStack() as pp:
                PM = [pst(pp, f"PMf{i}" + sfx, [128, 512]) for i in range(8)]
                PRa, bPRa = PM[4]
                PRb, bPRb = PM[5]
                XBT, bXBT = sbt(pp, "XBTf" + sfx, [128, 512], BF16)
                XSQ, bXSQ = sbt(pp, "XSQf" + sfx, [128, 512], F16)
                MUR, bMUR = sbt(pp, "MURf" + sfx, [128, 512], F32)
                RPR, bRPR = sbt(pp, "RPRf" + sfx, [128, 512], F32)
                SC0, bSC0 = sbt(pp, "SC0f" + sfx, [128, 512], F32)
                SC1, bSC1 = sbt(pp, "SC1f" + sfx, [128, 512], F32)
                BIGB, bBIGB = sbt(pp, "BIGBf" + sfx, [128, 8192], BF16)
                W1SL0, bW1SL0 = sbt(pp, "W1SL0" + sfx, [128, 1024], BF16)
                W1SL1, bW1SL1 = sbt(pp, "W1SL1" + sfx, [128, 1024], BF16)
                W2SL0, bW2SL0 = sbt(pp, "W2SL0" + sfx, [128, 1024], BF16)
                W2SL1, bW2SL1 = sbt(pp, "W2SL1" + sfx, [128, 1024], BF16)
                HEV = [sbt(pp, f"HEVf{i}" + sfx, [128, 512], F16) for i in range(4)]
                RLD0, bRLD0 = sbt(pp, "RLD0" + sfx, [128, 512], F16)
                RLD1, bRLD1 = sbt(pp, "RLD1" + sfx, [128, 512], F16)
                RHQ = [sbt(pp, f"RHQf{i}" + sfx, [128, 512], BF16) for i in range(2)]
                QU8, bQU8 = sbt(pp, "QU8f" + sfx, [128, 512], U8)
                HMX, bHMX = sbt(pp, "HMXf" + sfx, [128, 200], F32)
                bHSP = [[Buf() for _ in range(32)] for _ in range(8)]
                MAXC, bMAXC = sbt(pp, "MAXCf" + sfx, [128, 1], F32)
                print("ffn: sbuf left:", nc.sbuf_bytes_remaining)

                ln_stats(X, bX, PRa, bPRa, PRb, bPRb, XBT, bXBT, XSQ, bXSQ, NTOK)
                for blk in range(8):
                    replicate(RMU, blk, 1, MUR, bMUR, 128, PRa, bPRa, 512)
                    replicate(RRH, blk, 1, RPR, bRPR, 128, PRa, bPRa, 512)
                    for c in range(NCH):
                        V(lambda e, c=c, blk=blk: e.tensor_tensor(
                            SC0[:], X[c][:, 512 * blk:512 * (blk + 1)], MUR[:], ALU.subtract),
                          r=[bX[c], bMUR], w=[bSC0])
                        V(lambda e: e.tensor_tensor(SC1[:], SC0[:], RPR[:], ALU.mult),
                          r=[bSC0, bRPR], w=[bSC1])
                        V(lambda e, c=c, blk=blk: e.reduce_max(HMX[:, 8 * c + blk:8 * c + blk + 1],
                                                               SC1[:], axis=AX.X),
                          r=[bSC1], w=[bHMX])
                V(lambda e: e.reduce_max(MAXC[:], HMX[:, 0:64], axis=AX.X), r=[bHMX], w=[bMAXC])
                cross_part_max(MAXC[:], bMAXC, PRa, bPRa, 0)
                s.op("sync", lambda e, k=cc_idx0: e.dma_start(out=ccin[k], in_=SC11[0:1, 0:1]),
                     reads=[bSC11], writes=[bSC11], dma=True, chan="qc")
                G(lambda e, k=cc_idx0: e.collective_compute(
                    "AllReduce", ALU.max, replica_groups=[[0, 1, 2, 3, 4, 5, 6, 7]],
                    ins=[ccin[k]], outs=[ccout[k]]), r=[bSC11], w=[bSC11])
                s.op("sync", lambda e, k=cc_idx0: e.dma_start(out=SC11[0:1, 0:1], in_=ccout[k]),
                     reads=[bSC11], writes=[bSC11], dma=True, chan="qc")
                V(lambda e: e.tensor_scalar(SC11[0:1, 1:2], SC11[0:1, 0:1], float(Q_EPS), 1.0 / 255.0,
                                            ALU.max, ALU.mult), r=[bSC11], w=[bSC11])
                V(lambda e: e.reciprocal(SC11[0:1, 2:3], SC11[0:1, 1:2]), r=[bSC11], w=[bSC11])
                V(lambda e: e.tensor_scalar(SCW[0:1, :], ONESR_F[0:1, :], SC11[0:1, 1:2], None,
                                            ALU.mult), r=[bONESR_F, bSC11], w=[bSCW])
                TT(lambda e: e.matmul(PRa[:, 0:1], SCW[0:1, :], ONESR_F[0:1, 0:1],
                                      start=True, stop=True), r=[bONESR_F, bSCW], w=[bPRa])
                V(lambda e: e.tensor_copy(S1V[:, 0:1], PRa[:, 0:1]), r=[bPRa], w=[bS1V])
                V(lambda e: e.tensor_scalar(SCW[0:1, :], ONESR_F[0:1, :], SC11[0:1, 2:3], None,
                                            ALU.mult), r=[bONESR_F, bSC11], w=[bSCW])
                TT(lambda e: e.matmul(PRa[:, 0:1], SCW[0:1, :], ONESR_F[0:1, 0:1],
                                      start=True, stop=True), r=[bONESR_F, bSCW], w=[bPRa])
                V(lambda e: e.tensor_copy(S1V[:, 1:2], PRa[:, 0:1]), r=[bPRa], w=[bS1V])
                for i in range(2):
                    r2v, r2b = sview(RR2, i)
                    rhv, rhb = sview(RRH, i)
                    V(lambda e, r2v=r2v, rhv=rhv: e.tensor_scalar(r2v, rhv, S1V[0:97, 1:2], None,
                                                                  ALU.mult),
                      r=[rhb, bS1V], w=[r2b])
                V(lambda e: e.tensor_scalar(SV1[:], SW1[:], S1V[:, 0:1], None, ALU.mult),
                  r=[bSW1, bS1V], w=[bSV1])
                V(lambda e: e.reciprocal(SC0[:, 0:32], SV1[:]), r=[bSV1], w=[bSC0])
                V(lambda e: e.tensor_tensor(SC1[:, 0:32], B1S[:], SC0[:, 0:32], ALU.mult),
                  r=[bB1S, bSC0], w=[bSC1])
                V(lambda e: e.tensor_copy(SCI1[:], SC1[:, 0:32]), r=[bSC1], w=[bSCI1])
                V(lambda e: e.tensor_tensor(BQ1[:], SCI1[:], SV1[:], ALU.mult),
                  r=[bSCI1, bSV1], w=[bBQ1])
                for qt in range(4):
                    tq0 = 1024 * qt
                    for blk in range(2):
                        gblk = 2 * qt + blk
                        replicate(RMU, gblk, 1, MUR, bMUR, 128, PRa, bPRa, 512)
                        replicate(RR2, gblk, 1, RPR, bRPR, 128, PRa, bPRa, 512)
                        for c in range(NCH):
                            V(lambda e, c=c, gblk=gblk: e.tensor_tensor(
                                SC0[:], X[c][:, 512 * gblk:512 * (gblk + 1)], MUR[:], ALU.subtract),
                              r=[bX[c], bMUR], w=[bSC0])
                            V(lambda e: e.tensor_tensor(QU8[:], SC0[:], RPR[:], ALU.mult),
                              r=[bSC0, bRPR], w=[bQU8])
                            V(lambda e, c=c, blk=blk: e.tensor_copy(
                                BIGB[:, 1024 * c + 512 * blk:1024 * c + 512 * (blk + 1)], QU8[:]),
                              r=[bQU8], w=[bBIGB])
                    for fc in range(32):
                        WS, bWS = (W1SL0, bW1SL0) if fc % 2 == 0 else (W1SL1, bW1SL1)
                        QL(lambda e, fc=fc, WS=WS: e.dma_start(out=WS[:], in_=wint1t[fc]),
                          r=[bWS], w=[bWS])
                        for blk in range(2):
                            pm, bpm = PM[(fc * 2 + blk) % 4]
                            for dc in range(NCH):
                                TT(lambda e, blk=blk, dc=dc, pm=pm, WS=WS: e.matmul(
                                    pm[:, 0:512], WS[:, 128 * dc:128 * (dc + 1)],
                                    BIGB[:, 1024 * dc + 512 * blk:1024 * dc + 512 * (blk + 1)],
                                    start=(dc == 0), stop=(dc == 7)),
                                   r=[bWS, bBIGB], w=[bpm])
                            hv, bhv = HEV[(fc * 2 + blk) % 4]
                            tb = 2 * qt + blk
                            V(lambda e, fc=fc, pm=pm: e.tensor_scalar(
                                SC1[:], pm[:, 0:512], SV1[:, fc:fc + 1], BQ1[:, fc:fc + 1],
                                ALU.mult, ALU.add), r=[bpm, bSV1, bBQ1], w=[bSC1])
                            midx = fc * 2 + blk
                            V(lambda e, hv=hv, midx=midx: e.tensor_scalar(
                                hv[:], SC1[:], 0.0, -1e30, ALU.max, ALU.max,
                                accum_out=HMX[:, 64 + midx:65 + midx]),
                              r=[bSC1], w=[bhv, bHMX])
                            s.op("scalar", lambda e, tb=tb, fc=fc, hv=hv: e.dma_start(
                                out=hsp[tb, fc], in_=hv[:]),
                                 reads=[bhv], writes=[bhv, bHSP[tb][fc]], dma=True, chan="qs")
                    V(lambda e, qt=qt: e.reduce_max(HMX[:, 192 + qt:193 + qt], HMX[:, 64:128],
                                                    axis=AX.X), r=[bHMX], w=[bHMX])
                    s.barrier()
                V(lambda e: e.reduce_max(MAXC[:], HMX[:, 192:196], axis=AX.X), r=[bHMX], w=[bMAXC])
                cross_part_max(MAXC[:], bMAXC, PRa, bPRa, 0)
                s.op("sync", lambda e, k=cc_idx0 + 1: e.dma_start(out=ccin[k], in_=SC11[0:1, 0:1]),
                     reads=[bSC11], writes=[bSC11], dma=True, chan="qc")
                G(lambda e, k=cc_idx0 + 1: e.collective_compute(
                    "AllReduce", ALU.max, replica_groups=[[0, 1, 2, 3, 4, 5, 6, 7]],
                    ins=[ccin[k]], outs=[ccout[k]]), r=[bSC11], w=[bSC11])
                s.op("sync", lambda e, k=cc_idx0 + 1: e.dma_start(out=SC11[0:1, 0:1], in_=ccout[k]),
                     reads=[bSC11], writes=[bSC11], dma=True, chan="qc")
                V(lambda e: e.tensor_scalar(SC11[0:1, 1:2], SC11[0:1, 0:1], float(Q_EPS), 1.0 / 255.0,
                                            ALU.max, ALU.mult), r=[bSC11], w=[bSC11])
                V(lambda e: e.reciprocal(SC11[0:1, 2:3], SC11[0:1, 1:2]), r=[bSC11], w=[bSC11])
                V(lambda e: e.tensor_scalar(SCW[0:1, :], ONESR_F[0:1, :], SC11[0:1, 1:2], None,
                                            ALU.mult), r=[bONESR_F, bSC11], w=[bSCW])
                TT(lambda e: e.matmul(PRa[:, 0:1], SCW[0:1, :], ONESR_F[0:1, 0:1],
                                      start=True, stop=True), r=[bONESR_F, bSCW], w=[bPRa])
                V(lambda e: e.tensor_copy(S1V[:, 0:1], PRa[:, 0:1]), r=[bPRa], w=[bS1V])
                V(lambda e: e.tensor_scalar(SCW[0:1, :], ONESR_F[0:1, :], SC11[0:1, 2:3], None,
                                            ALU.mult), r=[bONESR_F, bSC11], w=[bSCW])
                TT(lambda e: e.matmul(PRa[:, 0:1], SCW[0:1, :], ONESR_F[0:1, 0:1],
                                      start=True, stop=True), r=[bONESR_F, bSCW], w=[bPRa])
                V(lambda e: e.tensor_copy(S1V[:, 1:2], PRa[:, 0:1]), r=[bPRa], w=[bS1V])
                V(lambda e: e.tensor_scalar(SV2[:], SW2[:], S1V[:, 0:1], None, ALU.mult),
                  r=[bSW2, bS1V], w=[bSV2])
                V(lambda e: e.reciprocal(SC0[:, 0:8], SV2[:]), r=[bSV2], w=[bSC0])
                V(lambda e: e.tensor_tensor(SC1[:, 0:8], B2S[:], SC0[:, 0:8], ALU.mult),
                  r=[bB2S, bSC0], w=[bSC1])
                V(lambda e: e.tensor_copy(SCI1[:, 0:8], SC1[:, 0:8]), r=[bSC1], w=[bSCI1])
                V(lambda e: e.tensor_tensor(BQ2[:], SCI1[:, 0:8], SV2[:], ALU.mult),
                  r=[bSCI1, bSV2], w=[bBQ2])
                s.barrier()
                if DEBUG_H0 and cc_idx0 == 0:
                    for tb_ in range(8):
                        for fc in range(32):
                            Q(lambda e, tb_=tb_, fc=fc: e.dma_start(out=dhsp[tb_, fc], in_=hsp[tb_, fc]),
                              r=[bHSP[tb_][fc]], w=[bHSP[tb_][fc]])
                for tb in range(8):
                    s.barrier()
                    for fc in range(32):
                        RLD, bRLD = (RLD0, bRLD0) if fc % 2 == 0 else (RLD1, bRLD1)
                        rhq, brhq = RHQ[fc % 2]
                        WS, bWS = (W2SL0, bW2SL0) if fc % 2 == 0 else (W2SL1, bW2SL1)
                        s.op("sync", lambda e, tb=tb, fc=fc, RLD=RLD: e.dma_start(
                            out=RLD[:], in_=hsp[tb, fc]),
                             reads=[bRLD, bHSP[tb][fc]], writes=[bRLD], dma=True, chan="qr")
                        QL(lambda e, fc=fc, WS=WS: e.dma_start(out=WS[:], in_=wint2t[fc]),
                          r=[bWS], w=[bWS])
                        V(lambda e, RLD=RLD: e.tensor_scalar(QU8[:], RLD[:], S1V[:, 1:2], None,
                                                             ALU.mult),
                          r=[bRLD, bS1V], w=[bQU8])
                        V(lambda e, rhq=rhq: e.tensor_copy(rhq[:], QU8[:]), r=[bQU8], w=[brhq])
                        if DEBUG_H0 and cc_idx0 == 0:
                            Q(lambda e, tb=tb, fc=fc, rhq=rhq: e.dma_start(out=drh[tb, fc], in_=rhq[:]),
                              r=[brhq], w=[brhq])
                        for dc in range(NCH):
                            pm, bpm = PM[dc]
                            TT(lambda e, fc=fc, dc=dc, pm=pm, WS=WS, rhq=rhq: e.matmul(
                                pm[:, 0:512], WS[:, 128 * dc:128 * (dc + 1)], rhq[:],
                                start=(fc == 0), stop=(fc == 31)),
                               r=[bWS, brhq], w=[bpm])
                    for dc in range(NCH):
                        pm, bpm = PM[dc]
                        V(lambda e, dc=dc, pm=pm: e.tensor_scalar(
                            SC1[:], pm[:, 0:512], SV2[:, dc:dc + 1], BQ2[:, dc:dc + 1],
                            ALU.mult, ALU.add), r=[bpm, bSV2, bBQ2], w=[bSC1])
                        if DEBUG_H0 and cc_idx0 == 0:
                            Q(lambda e, tb=tb, dc=dc: e.dma_start(out=dbr[tb, dc], in_=SC1[:]),
                              r=[bSC1], w=[bSC1])
                        V(lambda e, dc=dc, tb=tb: e.tensor_tensor(
                            X[dc][:, 512 * tb:512 * (tb + 1)], X[dc][:, 512 * tb:512 * (tb + 1)],
                            SC1[:], ALU.add), r=[bSC1, bX[dc]], w=[bX[dc]])
            s.barrier()

        def emit_attn_b():
            with contextlib.ExitStack() as pp:
                PS1, bPS1 = pst(pp, "PS1b", [128, 1024])
                PS2, bPS2 = pst(pp, "PS2b", [128, 1024])
                PC, bPC = pst(pp, "PCb", [65, 2048])
                XS = []
                bXS = []
                for c in range(NCH):
                    tt_, bb_ = sbt(pp, f"XSb{c}", [128, T], F32)
                    XS.append(tt_)
                    bXS.append(bb_)
                XBT, bXBT = sbt(pp, "XBTb", [128, 512], BF16)
                XSQ, bXSQ = sbt(pp, "XSQb", [128, 512], F16)
                ZNB, bZNB = sbt(pp, "ZNBb", [128, 16384], BF16)
                CTXB, bCTXB = sbt(pp, "CTXBb", [128, 16384], BF16)
                WPS0, bWPS0 = sbt(pp, "WPS0b", [128, 1024], BF16)
                WPS1, bWPS1 = sbt(pp, "WPS1b", [128, 1024], BF16)
                QKVW0, bQKVW0 = sbt(pp, "QKVW0b", [128, 1024], BF16)
                QKVW1, bQKVW1 = sbt(pp, "QKVW1b", [128, 1024], BF16)
                MU32, bMU32 = sbt(pp, "MU32b", [128, 512], F32)
                R132, bR132 = sbt(pp, "R132b", [128, 512], F32)
                MUB, bMUB = sbt(pp, "MUBb", [128, 2048], BF16)
                RHOB, bRHOB = sbt(pp, "RHOBb", [128, 2048], BF16)
                QA, bQA, KA, bKA = [], [], [], []
                for i in range(2):
                    tq, bq_ = sbt(pp, f"QAb{i}", [65, 2048], BF16)
                    QA.append(tq)
                    bQA.append(bq_)
                    tk, bk_ = sbt(pp, f"KAb{i}", [65, 2048], BF16)
                    KA.append(tk)
                    bKA.append(bk_)
                AT0, bAT0 = sbt(pp, "AT0b", [128, 1024], BF16)
                AT1, bAT1 = sbt(pp, "AT1b", [128, 1024], BF16)
                RSUM, bRSUM = sbt(pp, "RSUMb", [64, 1024], F32)
                SC0, bSC0 = sbt(pp, "SC0b", [128, 512], F32)
                SC1, bSC1 = sbt(pp, "SC1b", [128, 512], F32)
                print("attn_b: sbuf left:", nc.sbuf_bytes_remaining)

                V(lambda e: e.memset(KA[0][64:65, :], 1.0), r=[bKA[0]], w=[bKA[0]])
                V(lambda e: e.memset(KA[1][64:65, :], 1.0), r=[bKA[1]], w=[bKA[1]])

                for b in range(NSEQ):
                    t0 = b * T
                    for c in range(NCH):
                        Q(lambda e, c=c, t0=t0: e.dma_start(out=XS[c][:], in_=xsp[c][:, t0:t0 + T]),
                          r=[bXS[c]], w=[bXS[c]])
                    ln_stats(XS, bXS, PS1, bPS1, PS2, bPS2, XBT, bXBT, XSQ, bXSQ, T)
                    scv, scb = sview(RSC, 0)
                    r1v, r1b = sview(RRH, 0)
                    vrv, vrb = sview(RVR, 0)
                    r2v, r2b = sview(RR2, 0)
                    V(lambda e: e.tensor_tensor(scv, r1v, r1v, ALU.mult), r=[r1b], w=[scb])
                    V(lambda e: e.tensor_tensor(scv, scv, vrv, ALU.mult), r=[scb, vrb], w=[scb])
                    V(lambda e: e.tensor_scalar(scv, scv, float(LN_EPS), None, ALU.add),
                      r=[scb], w=[scb])
                    SS(lambda e: e.activation(scv, scv, AF.Ln), r=[scb], w=[scb])
                    SS(lambda e: e.activation(scv, scv, AF.Exp, scale=-0.5), r=[scb], w=[scb])
                    V(lambda e: e.tensor_tensor(r2v, r1v, scv, ALU.mult), r=[r1b, scb], w=[r2b])
                    replicate(RMU, 0, 4, MUB, bMUB, 128, PS1, bPS1, 1024)
                    replicate(RR2, 0, 4, RHOB, bRHOB, 128, PS1, bPS1, 1024)
                    for c in range(NCH):
                        for sg in range(4):
                            V(lambda e, c=c, sg=sg: e.tensor_tensor(
                                SC0[:], XS[c][:, 512 * sg:512 * (sg + 1)],
                                MUB[:, 512 * sg:512 * (sg + 1)], ALU.subtract),
                              r=[bXS[c], bMUB], w=[bSC0])
                            V(lambda e, sg=sg: e.tensor_tensor(
                                SC1[:], SC0[:], RHOB[:, 512 * sg:512 * (sg + 1)], ALU.mult),
                              r=[bSC0, bRHOB], w=[bSC1])
                            V(lambda e, c=c, sg=sg: e.tensor_scalar(
                                ZNB[:, 2048 * c + 512 * sg:2048 * c + 512 * (sg + 1)], SC1[:],
                                LNS[:, c:c + 1], LNB[:, c:c + 1], ALU.mult, ALU.add),
                              r=[bSC1, bLNS, bLNB], w=[bZNB])
                    for p in range(8):
                        for g in range(3):
                            QW, bQW = (QKVW0, bQKVW0) if g % 2 == 0 else (QKVW1, bQKVW1)
                            for dc in range(NCH):
                                QL(lambda e, g=g, p=p, dc=dc, QW=QW: e.dma_start(
                                    out=QW[:, 128 * dc:128 * (dc + 1)],
                                    in_=qkvb[g, p, 128 * dc:128 * (dc + 1), :]),
                                  r=[bQW], w=[bQW])
                            for sg in range(4):
                                PSs, bPSs = (PS1, bPS1) if sg % 2 == 0 else (PS2, bPS2)
                                for dc in range(NCH):
                                    TT(lambda e, sg=sg, dc=dc, PSs=PSs, QW=QW: e.matmul(
                                        PSs[:, 0:512],
                                        QW[:, 128 * dc:128 * (dc + 1)],
                                        ZNB[:, 2048 * dc + 512 * sg:2048 * dc + 512 * (sg + 1)],
                                        start=(dc == 0), stop=(dc == 7)),
                                       r=[bQW, bZNB], w=[bPSs])
                                if g == 0:
                                    for hh in range(2):
                                        V(lambda e, sg=sg, hh=hh, PSs=PSs: e.tensor_copy(
                                            QA[hh][0:64, 512 * sg:512 * (sg + 1)],
                                            PSs[64 * hh:64 * hh + 64, 0:512]),
                                          r=[bPSs], w=[bQA[hh]])
                                elif g == 1:
                                    for hh in range(2):
                                        V(lambda e, sg=sg, hh=hh, PSs=PSs: e.tensor_copy(
                                            KA[hh][0:64, 512 * sg:512 * (sg + 1)],
                                            PSs[64 * hh:64 * hh + 64, 0:512]),
                                          r=[bPSs], w=[bKA[hh]])
                                else:
                                    V(lambda e, p=p, sg=sg, PSs=PSs: e.tensor_copy(
                                        CTXB[:, 2048 * p + 512 * sg:2048 * p + 512 * (sg + 1)],
                                        PSs[:, 0:512]),
                                      r=[bPSs], w=[bCTXB])
                        for hh in range(2):
                            hr0 = 64 * hh
                            for tt_ in range(16):
                                QV(lambda e, p=p, tt_=tt_, hr0=hr0: e.dma_start_transpose(
                                    VTOK[:, 80 * tt_:80 * tt_ + 64],
                                    CTXB[hr0:hr0 + 64, 2048 * p + 128 * tt_:2048 * p + 128 * (tt_ + 1)]),
                                  r=[bCTXB, bVTOK], w=[bVTOK])
                            for sg in range(4):
                                o = 512 * (sg % 2)
                                V(lambda e, sg=sg, hh=hh: e.tensor_tensor(
                                    XSQ[0:64, 0:512], KA[hh][0:64, 512 * sg:512 * (sg + 1)],
                                    KA[hh][0:64, 512 * sg:512 * (sg + 1)], ALU.mult),
                                  r=[bKA[hh], bXSQ], w=[bXSQ])
                                TT(lambda e, o=o: e.matmul(PS1[0:1, o:o + 512], ONESC_H[0:64, :],
                                                           XSQ[0:64, 0:512], start=True, stop=True),
                                   r=[bONESC_H, bXSQ], w=[bPS1])
                                V(lambda e, sg=sg, o=o: e.reduce_max(
                                    SC11[0:1, sg:sg + 1], PS1[0:1, o:o + 512], axis=AX.X),
                                  r=[bPS1], w=[bSC11])
                            V(lambda e: e.reduce_max(SC11[0:1, 4:5], SC11[0:1, 0:4], axis=AX.X),
                              r=[bSC11], w=[bSC11])
                            SS(lambda e: e.activation(SC11[0:1, 4:5], SC11[0:1, 4:5], AF.Ln),
                               r=[bSC11], w=[bSC11])
                            SS(lambda e: e.activation(SC11[0:1, 4:5], SC11[0:1, 4:5], AF.Exp,
                                                      scale=0.5), r=[bSC11], w=[bSC11])
                            TT(lambda e: e.matmul(PS1[:, 0:1], ONESR_F[0:1, :], SC11[0:1, 4:5],
                                                  start=True, stop=True),
                               r=[bONESR_F, bSC11], w=[bPS1])
                            V(lambda e: e.tensor_copy(BIASV[:], PS1[:, 0:1]), r=[bPS1], w=[bBIASV])
                            for sg in range(4):
                                V(lambda e, sg=sg, hh=hh: e.tensor_tensor(
                                    XSQ[0:64, 0:512], QA[hh][0:64, 512 * sg:512 * (sg + 1)],
                                    QA[hh][0:64, 512 * sg:512 * (sg + 1)], ALU.mult),
                                  r=[bQA[hh], bXSQ], w=[bXSQ])
                                TT(lambda e, sg=sg: e.matmul(PS1[32 * sg:32 * sg + 1, 0:512],
                                                             ONESC_H[0:64, :], XSQ[0:64, 0:512],
                                                             start=True, stop=True,
                                                             tile_position=(0, 32 * sg)),
                                   r=[bONESC_H, bXSQ], w=[bPS1])
                            V(lambda e: e.tensor_copy(RSC[0][0][0:97, :], PS1[0:97, 0:512]),
                              r=[bPS1], w=[RSC[0][1]])
                            scv2, scb2 = sview(RSC, 0)
                            SS(lambda e: e.activation(scv2, scv2, AF.Ln), r=[scb2], w=[scb2])
                            SS(lambda e: e.activation(scv2, scv2, AF.Exp, scale=0.5),
                               r=[scb2], w=[scb2])
                            for sg in range(4):
                                V(lambda e, sg=sg, hh=hh: e.tensor_scalar(
                                    QA[hh][64:65, 512 * sg:512 * (sg + 1)], srow(RSC, sg),
                                    BIASV[32 * sg:32 * sg + 1, 0:1], -1.0, ALU.mult, ALU.mult),
                                  r=[RSC[0][1], bBIASV], w=[bQA[hh]])
                            for qh in range(2):
                                qo = 1024 * qh
                                for kb in range(16):
                                    PSs, bPSs, ATs, bATs = (PS1, bPS1, AT0, bAT0) if kb % 2 == 0 \
                                        else (PS2, bPS2, AT1, bAT1)
                                    for sg in range(2):
                                        TT(lambda e, kb=kb, qo=qo, sg=sg, hh=hh, PSs=PSs: e.matmul(
                                            PSs[:, 512 * sg:512 * (sg + 1)],
                                            KA[hh][0:65, 128 * kb:128 * (kb + 1)],
                                            QA[hh][0:65, qo + 512 * sg:qo + 512 * (sg + 1)],
                                            start=True, stop=True),
                                           r=[bKA[hh], bQA[hh]], w=[bPSs])
                                    SS(lambda e, PSs=PSs, ATs=ATs: e.activation(ATs[:], PSs[:], AF.Exp),
                                       r=[bPSs], w=[bATs])
                                    for sg in range(2):
                                        TT(lambda e, kb=kb, qo=qo, sg=sg, ATs=ATs: e.matmul(
                                            PC[0:65, qo + 512 * sg:qo + 512 * (sg + 1)],
                                            VTOK[:, 80 * kb:80 * kb + 65],
                                            ATs[:, 512 * sg:512 * (sg + 1)],
                                            start=(kb == 0), stop=(kb == 15)),
                                           r=[bVTOK, bATs], w=[bPC])
                            for sg in range(4):
                                V(lambda e, sg=sg: e.reciprocal(srow(RSC, sg),
                                                                PC[64:65, 512 * sg:512 * (sg + 1)]),
                                  r=[bPC], w=[RSC[0][1]])
                            for sg in range(2):
                                replicate(RSC, 2 * sg, 2, RSUM, bRSUM, 64, PS1, bPS1, 1024)
                                V(lambda e, p=p, sg=sg, hr0=hr0: e.tensor_tensor(
                                    CTXB[hr0:hr0 + 64, 2048 * p + 1024 * sg:2048 * p + 1024 * (sg + 1)],
                                    PC[0:64, 1024 * sg:1024 * (sg + 1)],
                                    RSUM[:, 0:1024], ALU.mult),
                                  r=[bPC, bRSUM], w=[bCTXB])
                    for sg in range(4):
                        replicate(RMU, sg, 1, MU32, bMU32, 128, PS1, bPS1, 512)
                        replicate(RRH, sg, 1, R132, bR132, 128, PS1, bPS1, 512)
                        for oc in range(NCH):
                            WP, bWP = (WPS0, bWPS0) if oc % 2 == 0 else (WPS1, bWPS1)
                            QL(lambda e, oc=oc, WP=WP: e.dma_start(out=WP[:], in_=wprojb[oc]),
                              r=[bWP], w=[bWP])
                            pm, bpm = (PS1, bPS1) if oc % 2 == 0 else (PS2, bPS2)
                            for dc in range(NCH):
                                TT(lambda e, sg=sg, dc=dc, pm=pm, WP=WP: e.matmul(
                                    pm[:, 0:512],
                                    WP[:, 128 * dc:128 * (dc + 1)],
                                    CTXB[:, 2048 * dc + 512 * sg:2048 * dc + 512 * (sg + 1)],
                                    start=(dc == 0), stop=(dc == 7)),
                                   r=[bWP, bCTXB], w=[bpm])
                            V(lambda e, oc=oc, sg=sg: e.tensor_tensor(
                                SC0[:], XS[oc][:, 512 * sg:512 * (sg + 1)],
                                MU32[:], ALU.subtract),
                              r=[bXS[oc], bMU32], w=[bSC0])
                            V(lambda e: e.tensor_tensor(
                                SC0[:], SC0[:], R132[:], ALU.mult),
                              r=[bSC0, bR132], w=[bSC0])
                            V(lambda e, oc=oc, pm=pm: e.scalar_tensor_tensor(
                                SC1[:], pm[:, 0:512], BPJ[:, oc:oc + 1], SC0[:], ALU.add, ALU.add),
                              r=[bpm, bBPJ, bSC0], w=[bSC1])
                            V(lambda e, oc=oc, sg=sg: e.tensor_tensor(
                                XS[oc][:, 512 * sg:512 * (sg + 1)], XS[oc][:, 512 * sg:512 * (sg + 1)],
                                SC1[:], ALU.add), r=[bSC1, bXS[oc]], w=[bXS[oc]])
                    for c in range(NCH):
                        Q(lambda e, c=c, t0=t0: e.dma_start(out=xsp[c][:, t0:t0 + T], in_=XS[c][:]),
                          r=[bXS[c]], w=[bXS[c]])
            s.barrier()

        def store_y(X, bX):
            for c in range(NCH):
                for b in range(NSEQ):
                    Q(lambda e, c=c, b=b: e.dma_start(out=y[b, 128 * c:128 * (c + 1), :],
                                                      in_=X[c][:, b * T:(b + 1) * T]),
                      r=[bX[c]], w=[bX[c]])

        with contextlib.ExitStack() as pxa:
            X, bX = [], []
            for c in range(NCH):
                tt_, bb_ = sbt(pxa, f"X{c}", [128, NTOK], F32)
                X.append(tt_)
                bX.append(bb_)
            for c in range(NCH):
                for b in range(NSEQ):
                    Q(lambda e, c=c, b=b: e.dma_start(out=X[c][:, b * T:(b + 1) * T],
                                                      in_=x_in[b, 128 * c:128 * (c + 1), :]),
                      r=[bX[c]], w=[bX[c]])
            if n_sub >= 1:
                emit_attn_a(X, bX)
            if n_sub >= 2:
                emit_ffn(X, bX, 0)
            if n_sub <= 2:
                store_y(X, bX)
            else:
                for c in range(NCH):
                    Q(lambda e, c=c: e.dma_start(out=xsp[c], in_=X[c][:]), r=[bX[c]], w=[bX[c]])
            s.barrier()

        if n_sub >= 3:
            emit_attn_b()
            with contextlib.ExitStack() as pxb:
                X2, bX2 = [], []
                for c in range(NCH):
                    tt_, bb_ = sbt(pxb, f"X2{c}", [128, NTOK], F32)
                    X2.append(tt_)
                    bX2.append(bb_)
                for c in range(NCH):
                    Q(lambda e, c=c: e.dma_start(out=X2[c][:], in_=xsp[c]), r=[bX2[c]], w=[bX2[c]])
                if n_sub >= 4:
                    emit_ffn(X2, bX2, 2)
                store_y(X2, bX2)
                s.barrier()

        s.barrier()
        with nc.Block() as block:
            s.finalize(block)
        tot = sum(len(v) for v in s.steps.values())
        print("total steps:", tot, {e: len(v) for e, v in s.steps.items()})
    return nc


_cache = {}


def kernel(**ins):
    f32 = np.float32
    x = np.ascontiguousarray(np.asarray(ins["L0_attn_a_x"], f32).transpose(0, 2, 1))
    wq = np.asarray(ins["L0_attn_b_w_qkv"], f32)
    wqkv_p = np.ascontiguousarray(wq.reshape(D, H, 64, 3).transpose(0, 3, 1, 2).reshape(D, 3 * D))
    common = {
        "ident": np.eye(128, dtype=f32),
        "wdq": np.ascontiguousarray(np.asarray(ins["L0_attn_a_W_DQ"], f32)),
        "wuqt": np.ascontiguousarray(np.asarray(ins["L0_attn_a_W_UQ"], f32).T),
        "wdkv": np.ascontiguousarray(np.asarray(ins["L0_attn_a_W_DKV"], f32)),
        "wukvt": np.ascontiguousarray(np.asarray(ins["L0_attn_a_W_UKV"], f32).T),
        "wqrt": np.ascontiguousarray(np.asarray(ins["L0_attn_a_W_QR"], f32).T),
        "wkrt": np.ascontiguousarray(np.asarray(ins["L0_attn_a_W_KR"], f32).T),
        "wot": np.ascontiguousarray(np.asarray(ins["L0_attn_a_W_O"], f32).T),
        "qnw": np.ascontiguousarray(np.asarray(ins["L0_attn_a_q_norm_w"], f32).reshape(64, 1)),
        "knw": np.ascontiguousarray(np.asarray(ins["L0_attn_a_k_norm_w"], f32).reshape(64, 1)),
        "lnsc": np.ascontiguousarray(np.asarray(ins["L0_attn_b_ln_scale"], f32).reshape(8, 128).T),
        "lnbi": np.ascontiguousarray(np.asarray(ins["L0_attn_b_ln_bias"], f32).reshape(8, 128).T),
        "bprojc": np.ascontiguousarray(np.asarray(ins["L0_attn_b_b_proj"], f32).reshape(8, 128).T),
        "wqkv": wqkv_p,
        "wproj": np.ascontiguousarray(np.asarray(ins["L0_attn_b_w_proj"], f32)),
        "w1": np.ascontiguousarray(np.asarray(ins["L0_ffn_w1"], f32)),
        "w1t": np.ascontiguousarray(np.asarray(ins["L0_ffn_w1"], f32).T),
        "b1c": np.ascontiguousarray(np.asarray(ins["L0_ffn_b1"], f32).reshape(32, 128).T),
        "w2": np.ascontiguousarray(np.asarray(ins["L0_ffn_w2"], f32)),
        "w2t": np.ascontiguousarray(np.asarray(ins["L0_ffn_w2"], f32).T),
        "b2c": np.ascontiguousarray(np.asarray(ins["L0_ffn_b2"], f32).reshape(8, 128).T),
    }
    key = (N_SUB, PARANOID, DEBUG_H0)
    if key not in _cache:
        _cache[key] = build_nc(N_SUB)
    nc = _cache[key]
    in_maps = []
    for i in range(8):
        m = dict(common)
        m["x_in"] = np.ascontiguousarray(x[2 * i:2 * i + 2])
        in_maps.append(m)
    res = run_bass_kernel_spmd(nc, in_maps, core_ids=list(range(8)))
    out = np.concatenate([np.asarray(r["y"]) for r in res.results], axis=0)
    return np.ascontiguousarray(out.transpose(0, 2, 1))


# version 05 (final)
# speedup vs baseline: 1.1839x; 1.1839x over previous
import contextlib
import numpy as np

import concourse.bass as bass
import concourse.mybir as mybir
from concourse.bass_utils import run_bass_kernel_spmd

F32 = mybir.dt.float32
BF16 = mybir.dt.bfloat16
F16 = mybir.dt.float16
U8 = mybir.dt.uint8
I16 = mybir.dt.int16
I32 = mybir.dt.int32
AF = mybir.ActivationFunctionType
ALU = mybir.AluOpType
AX = mybir.AxisListType

NSEQ = 2
T = 2048
NTOK = NSEQ * T
D = 1024
NCH = 8
H = 16
LN_EPS = 1e-5
RMS_EPS = 1.1920929e-07
Q_EPS = 1e-8

N_SUB = 4
PARANOID = 0
SAFE_SAME = True
DEBUG_H0 = False
ENGS = ("tensor", "vector", "scalar", "gpsimd", "sync")
CHANS = ENGS + ("qx", "qw", "qp", "qr", "qs", "qv", "qc")


class Buf:
    __slots__ = ("w", "r", "s")

    def __init__(self, s=False):
        self.w = None
        self.r = []
        self.s = s


class Sched:
    def __init__(self, nc, st):
        self.nc = nc
        self.sem = {c: st.enter_context(nc.semaphore("sem_" + c)) for c in CHANS}
        self.cnt = dict.fromkeys(CHANS, 0)
        self.steps = {e: [] for e in ENGS}
        self.waited = {e: dict.fromkeys(CHANS, 0) for e in ENGS}

    def op(self, eng, fn, reads=(), writes=(), dma=False, chan=None):
        if chan is None:
            chan = eng
        deps = []
        for b in reads:
            if b.w is not None:
                deps.append((b.w, b.s))
        for b in writes:
            if b.w is not None:
                deps.append((b.w, b.s))
            deps.extend((t, b.s) for t in b.r)
        waits = {}
        for ((pc, v, pdma), sm) in deps:
            if pc == eng and not pdma and not (SAFE_SAME and sm):
                continue
            if self.waited[eng][pc] >= v:
                continue
            if waits.get(pc, 0) < v:
                waits[pc] = v
        for pc, v in waits.items():
            self.waited[eng][pc] = v
        inc = 16 if dma else 1
        self.cnt[chan] += inc
        self.steps[eng].append((tuple(waits.items()), fn, inc, chan))
        tok = (chan, self.cnt[chan], dma)
        for b in reads:
            b.r = [t for t in b.r if t[0] != chan] + [tok]
        for b in writes:
            b.w = tok
            b.r = []
        return tok

    def barrier(self):
        snap = [(c, self.cnt[c], True) for c in CHANS]
        fakes = []
        for t in snap:
            fb = Buf()
            fb.w = t
            fakes.append(fb)
        for e in ENGS:
            self.op(e, lambda en: en.nop(), reads=fakes)

    def finalize(self, block):
        for e in ENGS:
            items = self.steps[e]
            CH = 1200
            for i0 in range(0, len(items), CH):
                chunk = items[i0:i0 + CH]

                def body(en, chunk=chunk, e=e):
                    for waits, fn, inc, chan in chunk:
                        for pc, v in waits:
                            en.wait_ge(self.sem[pc], v)
                        fn(en).then_inc(self.sem[chan], inc)

                getattr(block, e)(body)


def build_nc(n_sub=None, _unused=None):
    if n_sub is None:
        n_sub = N_SUB
    nc = bass.Bass("TRN2", target_bir_lowering=False)
    dt = nc.dram_tensor

    x_in = dt("x_in", [NSEQ, D, T], F32, kind="ExternalInput").ap()
    ident = dt("ident", [128, 128], F32, kind="ExternalInput").ap()
    wdq = dt("wdq", [512, 64], F32, kind="ExternalInput").ap()
    wuqt = dt("wuqt", [512, 64], F32, kind="ExternalInput").ap()
    wdkv = dt("wdkv", [1024, 64], F32, kind="ExternalInput").ap()
    wukvt = dt("wukvt", [1024, 128], F32, kind="ExternalInput").ap()
    wqrt = dt("wqrt", [64, 64], F32, kind="ExternalInput").ap()
    wkrt = dt("wkrt", [64, 64], F32, kind="ExternalInput").ap()
    wot = dt("wot", [64, 64], F32, kind="ExternalInput").ap()
    qnw = dt("qnw", [64, 1], F32, kind="ExternalInput").ap()
    knw = dt("knw", [64, 1], F32, kind="ExternalInput").ap()
    lnsc = dt("lnsc", [128, 8], F32, kind="ExternalInput").ap()
    lnbi = dt("lnbi", [128, 8], F32, kind="ExternalInput").ap()
    bprojc = dt("bprojc", [128, 8], F32, kind="ExternalInput").ap()
    wqkv = dt("wqkv", [1024, 3072], F32, kind="ExternalInput").ap()
    wproj = dt("wproj", [1024, 1024], F32, kind="ExternalInput").ap()
    w1 = dt("w1", [4096, 1024], F32, kind="ExternalInput").ap()
    w1t = dt("w1t", [1024, 4096], F32, kind="ExternalInput").ap()
    b1c = dt("b1c", [128, 32], F32, kind="ExternalInput").ap()
    w2 = dt("w2", [1024, 4096], F32, kind="ExternalInput").ap()
    w2t = dt("w2t", [4096, 1024], F32, kind="ExternalInput").ap()
    b2c = dt("b2c", [128, 8], F32, kind="ExternalInput").ap()

    y = dt("y", [NSEQ, D, T], F32, kind="ExternalOutput").ap()
    if DEBUG_H0:
        dqop = dt("dqop", [128, 2048], F32, kind="ExternalOutput").ap()
        dkop = dt("dkop", [128, 2048], F32, kind="ExternalOutput").ap()
        dbias = dt("dbias", [128, 1], F32, kind="ExternalOutput").ap()
        dsc = dt("dsc", [1, 16], F32, kind="ExternalOutput").ap()
        dat = dt("dat", [128, 1024], F32, kind="ExternalOutput").ap()
        drsum = dt("drsum", [64, 2048], F32, kind="ExternalOutput").ap()
        dhsp = dt("dhsp", [8, 32, 128, 512], F16, kind="ExternalOutput").ap()
        dbr = dt("dbr", [8, 8, 128, 512], F32, kind="ExternalOutput").ap()
        drh = dt("drh", [8, 32, 128, 512], BF16, kind="ExternalOutput").ap()

    qkvb = dt("qkvb", [3, 8, 1024, 128], BF16, kind="Internal").ap()
    wprojb = dt("wprojb", [8, 128, 1024], BF16, kind="Internal").ap()
    wint1t = dt("wint1t", [32, 128, 1024], BF16, kind="Internal").ap()
    wint2t = dt("wint2t", [32, 128, 1024], BF16, kind="Internal").ap()
    hsp = dt("hsp", [8, 32, 128, 512], F16, kind="Internal").ap()
    xsp = dt("xsp", [NCH, 128, NTOK], F32, kind="Internal").ap()
    ccin = [dt(f"ccin{i}", [1, 1], F32, kind="Internal").ap() for i in range(4)]
    ccout = [dt(f"ccout{i}", [1, 1], F32, kind="Internal").ap() for i in range(4)]

    with contextlib.ExitStack() as st:
        s = Sched(nc, st)

        def sbt(stk, name, shape, dtype, small=None):
            t = stk.enter_context(nc.sbuf_tensor(name, shape, dtype))
            if small is None:
                small = shape[0] * shape[1] <= 65536
            return t, Buf(small)

        def pst(stk, name, shape, dtype=F32):
            t = stk.enter_context(nc.psum_tensor(name, shape, dtype))
            return t, Buf()

        def V(fn, r=(), w=()):
            return s.op("vector", fn, r, w)

        def SS(fn, r=(), w=()):
            return s.op("scalar", fn, r, w)

        def TT(fn, r=(), w=()):
            return s.op("tensor", fn, r, w)

        def G(fn, r=(), w=()):
            return s.op("gpsimd", fn, r, w)

        def Q(fn, r=(), w=()):
            return s.op("sync", fn, r, w, dma=True, chan="qx")

        def QL(fn, r=(), w=()):
            return s.op("sync", fn, r, w, dma=True, chan="qw")

        def QP(fn, r=(), w=()):
            return s.op("sync", fn, r, w, dma=True, chan="qp")

        def QV(fn, r=(), w=()):
            return s.op("sync", fn, r, w, dma=True, chan="qv")

        def GD(fn, r=(), w=()):
            return s.op("gpsimd", fn, r, w, dma=True)

        IDF, bIDF = sbt(st, "IDF", [128, 128], F32)
        ONESR_F, bONESR_F = sbt(st, "ONESR_F", [1, 128], F32)
        ONESC_H, bONESC_H = sbt(st, "ONESC_H", [128, 1], F16)
        ONESC_B, bONESC_B = sbt(st, "ONESC_B", [128, 1], BF16)
        ZEROV, bZEROV = sbt(st, "ZEROV", [128, 1], F32)
        MASKB, bMASKB = sbt(st, "MASKB", [128, 128], BF16)
        MQT, bMQT = sbt(st, "MQT", [128, 64], BF16)
        NTT, bNTT = sbt(st, "NTT", [128, 128], BF16)
        WOVT, bWOVT = sbt(st, "WOVT", [128, 64], BF16)
        WQRP, bWQRP = sbt(st, "WQRP", [64, 64], BF16)
        WKRTB, bWKRTB = sbt(st, "WKRTB", [128, 64], BF16)
        WQK, bWQK = sbt(st, "WQK", [64, 1], F32)
        LNS, bLNS = sbt(st, "LNS", [128, 8], F32)
        LNB, bLNB = sbt(st, "LNB", [128, 8], F32)
        BPJ, bBPJ = sbt(st, "BPJ", [128, 8], F32)
        SW1, bSW1 = sbt(st, "SW1", [128, 32], F32)
        SW2, bSW2 = sbt(st, "SW2", [128, 8], F32)
        B1S, bB1S = sbt(st, "B1S", [128, 32], F32)
        B2S, bB2S = sbt(st, "B2S", [128, 8], F32)
        SV1, bSV1 = sbt(st, "SV1", [128, 32], F32)
        BQ1, bBQ1 = sbt(st, "BQ1", [128, 32], F32)
        SV2, bSV2 = sbt(st, "SV2", [128, 8], F32)
        BQ2, bBQ2 = sbt(st, "BQ2", [128, 8], F32)
        SCI1, bSCI1 = sbt(st, "SCI1", [128, 32], I32)
        RMU = [sbt(st, f"RMU{i}", [128, 512], F32, small=True) for i in range(2)]
        RRH = [sbt(st, f"RRH{i}", [128, 512], F32, small=True) for i in range(2)]
        RVR = [sbt(st, f"RVR{i}", [128, 512], F32, small=True) for i in range(2)]
        RR2 = [sbt(st, f"RR2{i}", [128, 512], F32, small=True) for i in range(2)]
        RSC = [sbt(st, f"RSC{i}", [128, 512], F32, small=True) for i in range(2)]
        SC11, bSC11 = sbt(st, "SC11", [1, 16], F32)
        BIASV, bBIASV = sbt(st, "BIASV", [128, 1], F32)
        S1V, bS1V = sbt(st, "S1V", [128, 2], F32)
        VTOK, bVTOK = sbt(st, "VTOK", [128, 16 * 80], BF16)
        ROWP, bROWP = sbt(st, "ROWP", [1, 128], F32)
        RROW512, bRROW512 = sbt(st, "RROW512", [1, 512], F32)
        SCW, bSCW = sbt(st, "SCW", [1, 128], F32)
        ONES4, bONES4 = sbt(st, "ONES4", [128, 128], F32)
        RROW0, bRROW0 = sbt(st, "RROW0", [1, 128], F32)

        try:
            nc.const_aps.register(F32, 0.0, ZEROV[:, 0:1])
        except Exception:
            pass

        def rsqrt_inplace(ap, buf, scale=-0.5):
            SS(lambda e, a=ap: e.activation(a, a, AF.Ln), r=[buf], w=[buf])
            SS(lambda e, a=ap, sc=scale: e.activation(a, a, AF.Exp, scale=sc), r=[buf], w=[buf])

        def srow(RT, j):
            t, _ = RT[j // 4]
            return t[32 * (j % 4):32 * (j % 4) + 1, :]

        def sbuf_of(RT, j):
            return RT[j // 4][1]

        def sview(RT, i):
            return RT[i][0][0:97, :], RT[i][1]

        def slab_op2(RTd, RTa, fn2, nhalf=2):
            for i in range(nhalf):
                dv, db = sview(RTd, i)
                av, ab = sview(RTa, i)
                V(lambda e, dv=dv, av=av: fn2(e, dv, av), r=[ab], w=[db])

        def cross_part_max(src_ap, src_buf, p_t, p_buf, dst_col):
            TT(lambda e: e.transpose(p_t[0:1, 0:128], src_ap, IDF[:]),
               r=[src_buf, bIDF], w=[p_buf])
            V(lambda e: e.tensor_copy(ROWP[0:1, 0:128], p_t[0:1, 0:128]), r=[p_buf], w=[bROWP])
            V(lambda e, dc_=dst_col: e.reduce_max(SC11[0:1, dc_:dc_ + 1], ROWP[0:1, 0:128],
                                                  axis=AX.X), r=[bROWP], w=[bSC11])

        def replicate(RT, j0, nblks, dst_ap, dst_buf, npart, PR, bPR, prw, scale=1.0):
            for jj in range(nblks):
                j = j0 + jj
                off = 512 * jj
                po = off % prw
                bp = 32 * (j % 4)
                if bp == 96:
                    V(lambda e, j=j: e.tensor_copy(RROW512[0:1, :], srow(RT, j)),
                      r=[sbuf_of(RT, j)], w=[bRROW512])
                    TT(lambda e, po=po: e.matmul(
                        PR[0:npart, po:po + 512], ONESR_F[0:1, 0:npart],
                        RROW512[0:1, :], start=True, stop=True),
                       r=[bONESR_F, bRROW512], w=[bPR])
                else:
                    TT(lambda e, j=j, po=po, bp=bp: e.matmul(
                        PR[0:npart, po:po + 512], ONES4[bp:bp + 1, 0:npart],
                        srow(RT, j), start=True, stop=True),
                       r=[bONES4, sbuf_of(RT, j)], w=[bPR])
                V(lambda e, off=off, po=po, sc=scale: e.tensor_scalar(
                    dst_ap[0:npart, off:off + 512], PR[0:npart, po:po + 512], float(sc), None, ALU.mult),
                  r=[bPR], w=[dst_buf])

        with contextlib.ExitStack() as p0:
            PSA, bPSA = pst(p0, "PSA0", [128, 512])
            PSB, bPSB = pst(p0, "PSB0", [128, 512])
            PSN, bPSN = pst(p0, "PSN0", [128, 512])

            QL(lambda e: e.dma_start(out=IDF[:], in_=ident), w=[bIDF])
            V(lambda e: e.memset(ONESR_F[:], 1.0), w=[bONESR_F])
            V(lambda e: e.memset(ONES4[:], 1.0), w=[bONES4])
            V(lambda e: e.memset(ONESC_H[:], 1.0), w=[bONESC_H])
            V(lambda e: e.memset(ONESC_B[:], 1.0), w=[bONESC_B])
            V(lambda e: e.memset(ZEROV[:], 0.0), w=[bZEROV])
            V(lambda e: e.memset(VTOK[:], 1.0), w=[bVTOK])
            V(lambda e: e.memset(MASKB[:], 1.0), w=[bMASKB])
            G(lambda e: e.affine_select(MASKB[:], MASKB[:], [[1, 128]], ALU.is_ge,
                                        0.0, base=0, channel_multiplier=-1),
              r=[bMASKB], w=[bMASKB])
            for src, dst, bdst in ((lnsc, LNS, bLNS), (lnbi, LNB, bLNB), (bprojc, BPJ, bBPJ),
                                   (b1c, B1S, bB1S), (b2c, B2S, bB2S)):
                QL(lambda e, d=dst, sp=src: e.dma_start(out=d[:], in_=sp), w=[bdst])

            SWDQ, bSWDQ = sbt(p0, "SWDQ", [128, 256], BF16)
            SWUQT, bSWUQT = sbt(p0, "SWUQT", [128, 256], BF16)
            SWDKV, bSWDKV = sbt(p0, "SWDKV", [128, 512], BF16)
            SWUKVT, bSWUKVT = sbt(p0, "SWUKVT", [128, 1024], BF16)
            NFUL, bNFUL = sbt(p0, "NFUL", [128, 64], BF16)
            SWOT, bSWOT = sbt(p0, "SWOT", [64, 64], BF16)
            SMALLF, bSMALLF = sbt(p0, "SMALLF", [128, 1024], F32)
            QNW, bQNW = sbt(p0, "QNW", [64, 1], F32)
            KNW, bKNW = sbt(p0, "KNW", [64, 1], F32)

            def load_chunked(dst, bdst, src, nch_, width):
                for j in range(nch_):
                    QL(lambda e, j=j, src=src, width=width: e.dma_start(
                        out=SMALLF[:, 0:width], in_=src[128 * j:128 * (j + 1), :]),
                      r=[bSMALLF], w=[bSMALLF])
                    V(lambda e, j=j, dst=dst, width=width: e.tensor_copy(
                        dst[:, width * j:width * (j + 1)], SMALLF[:, 0:width]),
                      r=[bSMALLF], w=[bdst])

            load_chunked(SWDQ, bSWDQ, wdq, 4, 64)
            load_chunked(SWUQT, bSWUQT, wuqt, 4, 64)
            load_chunked(SWDKV, bSWDKV, wdkv, 8, 64)
            load_chunked(SWUKVT, bSWUKVT, wukvt, 8, 128)
            QL(lambda e: e.dma_start(out=QNW[:], in_=qnw), w=[bQNW])
            QL(lambda e: e.dma_start(out=KNW[:], in_=knw), w=[bKNW])
            V(lambda e: e.tensor_tensor(WQK[:], QNW[:], KNW[:], ALU.mult), r=[bQNW, bKNW], w=[bWQK])
            QL(lambda e: e.dma_start(out=SMALLF[0:64, 0:64], in_=wot), r=[bSMALLF], w=[bSMALLF])
            V(lambda e: e.tensor_copy(SWOT[:], SMALLF[0:64, 0:64]), r=[bSMALLF], w=[bSWOT])

            for j in range(4):
                TT(lambda e, j=j: e.matmul(PSA[0:64, 0:64], SWDQ[:, 64 * j:64 * (j + 1)],
                                           SWUQT[:, 64 * j:64 * (j + 1)], start=(j == 0), stop=(j == 3)),
                   r=[bSWDQ, bSWUQT], w=[bPSA])
            V(lambda e: e.tensor_copy(MQT[0:64, :], PSA[0:64, 0:64]), r=[bPSA], w=[bMQT])
            V(lambda e: e.tensor_copy(MQT[64:128, :], MQT[0:64, :]), r=[bMQT], w=[bMQT])
            for j in range(8):
                TT(lambda e, j=j: e.matmul(PSB[0:128, 0:64], SWUKVT[:, 128 * j:128 * (j + 1)],
                                           SWDKV[:, 64 * j:64 * (j + 1)], start=(j == 0), stop=(j == 7)),
                   r=[bSWUKVT, bSWDKV], w=[bPSB])
            V(lambda e: e.tensor_copy(NFUL[:], PSB[0:128, 0:64]), r=[bPSB], w=[bNFUL])
            for j in range(8):
                TT(lambda e, j=j: e.matmul(PSA[0:64, 0:128], SWDKV[:, 64 * j:64 * (j + 1)],
                                           SWUKVT[:, 128 * j:128 * (j + 1)], start=(j == 0), stop=(j == 7)),
                   r=[bSWUKVT, bSWDKV], w=[bPSA])
            V(lambda e: e.tensor_copy(NTT[0:64, :], PSA[0:64, 0:128]), r=[bPSA], w=[bNTT])
            V(lambda e: e.tensor_copy(NTT[64:128, :], NTT[0:64, :]), r=[bNTT], w=[bNTT])
            V(lambda e: e.tensor_copy(NFUL[0:64, 0:64] if False else SWDQ[0:64, 0:64], NFUL[64:128, 0:64]),
              r=[bNFUL], w=[bSWDQ])
            TT(lambda e: e.matmul(PSN[0:64, 0:64], SWDQ[0:64, 0:64], SWOT[:], start=True, stop=True),
               r=[bSWDQ, bSWOT], w=[bPSN])
            V(lambda e: e.tensor_copy(WOVT[0:64, :], PSN[0:64, 0:64]), r=[bPSN], w=[bWOVT])
            V(lambda e: e.tensor_copy(WOVT[64:128, :], WOVT[0:64, :]), r=[bWOVT], w=[bWOVT])
            QL(lambda e: e.dma_start(out=SMALLF[0:64, 0:64], in_=wqrt), r=[bSMALLF], w=[bSMALLF])
            V(lambda e: e.tensor_scalar(WQRP[:], SMALLF[0:64, 0:64], QNW[:], None, ALU.mult),
              r=[bSMALLF, bQNW], w=[bWQRP])
            QL(lambda e: e.dma_start(out=SMALLF[0:64, 64:128], in_=wkrt), r=[bSMALLF], w=[bSMALLF])
            V(lambda e: e.tensor_copy(WKRTB[0:64, :], SMALLF[0:64, 64:128]), r=[bSMALLF], w=[bWKRTB])
            V(lambda e: e.tensor_copy(WKRTB[64:128, :], WKRTB[0:64, :]), r=[bWKRTB], w=[bWKRTB])

            WSTG, bWSTG = sbt(p0, "WSTG", [128, 1024], F32)
            WSTG2, bWSTG2 = sbt(p0, "WSTG2", [128, 1024], F32)
            WBF, bWBF = sbt(p0, "WBF", [128, 512], BF16)
            WI16, bWI16 = sbt(p0, "WI16", [128, 512], I16)
            RREC, bRREC = sbt(p0, "RREC", [128, 32], F32)
            REPG, bREPG = sbt(p0, "REPG", [128, 512], F32)
            MXW, bMXW = sbt(p0, "MXW", [128, 32], F32)

            for cb in range(6):
                for dc in range(8):
                    QL(lambda e, cb=cb, dc=dc: e.dma_start(
                        out=WSTG[:, 0:512], in_=wqkv[128 * dc:128 * (dc + 1), 512 * cb:512 * (cb + 1)]),
                      r=[bWSTG], w=[bWSTG])
                    sc = 0.125 if cb < 2 else 1.0
                    V(lambda e, sc=sc: e.tensor_scalar(WBF[:], WSTG[:, 0:512], float(sc), None, ALU.mult),
                      r=[bWSTG], w=[bWBF])
                    for jj in range(4):
                        col = cb * 512 + jj * 128
                        g, p = col // 1024, (col % 1024) // 128
                        QP(lambda e, g=g, p=p, dc=dc, jj=jj: e.dma_start(
                            out=qkvb[g, p, 128 * dc:128 * (dc + 1), :], in_=WBF[:, 128 * jj:128 * (jj + 1)]),
                          r=[bWBF], w=[bWBF])

            for dc in range(8):
                QL(lambda e, dc=dc: e.dma_start(out=WSTG[:, :], in_=wproj[128 * dc:128 * (dc + 1), :]),
                  r=[bWSTG], w=[bWSTG])
                for half in range(2):
                    V(lambda e, half=half: e.tensor_copy(WBF[:], WSTG[:, 512 * half:512 * (half + 1)]),
                      r=[bWSTG], w=[bWBF])
                    for jj in range(4):
                        oc = 4 * half + jj
                        QP(lambda e, oc=oc, jj=jj, dc=dc: e.dma_start(
                            out=wprojb[oc, :, 128 * dc:128 * (dc + 1)],
                            in_=WBF[:, 128 * jj:128 * (jj + 1)]),
                          r=[bWBF], w=[bWBF])
            for m in range(32):
                QL(lambda e, m=m: e.dma_start(out=WSTG2[:, :], in_=w1[128 * m:128 * (m + 1), :]),
                  r=[bWSTG2], w=[bWSTG2])
                V(lambda e, m=m: e.reduce_max(MXW[:, m:m + 1], WSTG2[:, :], axis=AX.X,
                                              apply_absolute_value=True), r=[bWSTG2], w=[bMXW])
            V(lambda e: e.tensor_scalar(SW1[:], MXW[:], float(Q_EPS), 1.0 / 127.0, ALU.max, ALU.mult),
              r=[bMXW], w=[bSW1])
            V(lambda e: e.reciprocal(RREC[:], SW1[:]), r=[bSW1], w=[bRREC])
            for g in range(8):
                for jj in range(4):
                    j = 4 * g + jj
                    TT(lambda e, j=j: e.transpose(PSA[0:1, 0:128], RREC[:, j:j + 1], IDF[:]),
                       r=[bRREC, bIDF], w=[bPSA])
                    V(lambda e: e.tensor_copy(RROW0[0:1, :], PSA[0:1, 0:128]),
                      r=[bPSA], w=[bRROW0])
                    TT(lambda e, jj=jj: e.matmul(PSB[:, 128 * jj:128 * (jj + 1)],
                                                 ONESR_F[0:1, :], RROW0[0:1, 0:128],
                                                 start=True, stop=True),
                       r=[bONESR_F, bRROW0], w=[bPSB])
                V(lambda e: e.tensor_copy(REPG[:], PSB[:, 0:512]), r=[bPSB], w=[bREPG])
                for dc in range(8):
                    QL(lambda e, g=g, dc=dc: e.dma_start(
                        out=WSTG[:, 0:512], in_=w1t[128 * dc:128 * (dc + 1), 512 * g:512 * (g + 1)]),
                      r=[bWSTG], w=[bWSTG])
                    V(lambda e: e.tensor_tensor(WI16[:], WSTG[:, 0:512], REPG[:], ALU.mult),
                      r=[bWSTG, bREPG], w=[bWI16])
                    V(lambda e: e.tensor_copy(WBF[:], WI16[:]), r=[bWI16], w=[bWBF])
                    for jj in range(4):
                        QP(lambda e, g=g, jj=jj, dc=dc: e.dma_start(
                            out=wint1t[4 * g + jj, :, 128 * dc:128 * (dc + 1)],
                            in_=WBF[:, 128 * jj:128 * (jj + 1)]),
                          r=[bWBF], w=[bWBF])

            for m in range(8):
                for qq in range(4):
                    QL(lambda e, m=m, qq=qq: e.dma_start(
                        out=WSTG2[:, :], in_=w2[128 * m:128 * (m + 1), 1024 * qq:1024 * (qq + 1)]),
                      r=[bWSTG2], w=[bWSTG2])
                    V(lambda e, qq=qq: e.reduce_max(MXW[:, qq:qq + 1], WSTG2[:, :], axis=AX.X,
                                                    apply_absolute_value=True), r=[bWSTG2], w=[bMXW])
                V(lambda e, m=m: e.reduce_max(MXW[:, 8 + m:9 + m], MXW[:, 0:4], axis=AX.X),
                  r=[bMXW], w=[bMXW])
            V(lambda e: e.tensor_scalar(SW2[:], MXW[:, 8:16], float(Q_EPS), 1.0 / 127.0, ALU.max, ALU.mult),
              r=[bMXW], w=[bSW2])
            V(lambda e: e.reciprocal(RREC[:, 0:8], SW2[:]), r=[bSW2], w=[bRREC])
            for g in range(2):
                for jj in range(4):
                    j = 4 * g + jj
                    TT(lambda e, j=j: e.transpose(PSA[0:1, 0:128], RREC[:, j:j + 1], IDF[:]),
                       r=[bRREC, bIDF], w=[bPSA])
                    V(lambda e: e.tensor_copy(RROW0[0:1, :], PSA[0:1, 0:128]),
                      r=[bPSA], w=[bRROW0])
                    TT(lambda e, jj=jj: e.matmul(PSB[:, 128 * jj:128 * (jj + 1)],
                                                 ONESR_F[0:1, :], RROW0[0:1, 0:128],
                                                 start=True, stop=True),
                       r=[bONESR_F, bRROW0], w=[bPSB])
                V(lambda e: e.tensor_copy(REPG[:], PSB[:, 0:512]), r=[bPSB], w=[bREPG])
                for fc in range(32):
                    QL(lambda e, g=g, fc=fc: e.dma_start(
                        out=WSTG[:, 0:512], in_=w2t[128 * fc:128 * (fc + 1), 512 * g:512 * (g + 1)]),
                      r=[bWSTG], w=[bWSTG])
                    V(lambda e: e.tensor_tensor(WI16[:], WSTG[:, 0:512], REPG[:], ALU.mult),
                      r=[bWSTG, bREPG], w=[bWI16])
                    V(lambda e: e.tensor_copy(WBF[:], WI16[:]), r=[bWI16], w=[bWBF])
                    for jj in range(4):
                        QP(lambda e, g=g, jj=jj, fc=fc: e.dma_start(
                            out=wint2t[fc, :, 128 * (4 * g + jj):128 * (4 * g + jj + 1)],
                            in_=WBF[:, 128 * jj:128 * (jj + 1)]),
                          r=[bWBF], w=[bWBF])
            s.barrier()
        s.barrier()
        print("after P0: sbuf bytes left:", nc.sbuf_bytes_remaining)

        def ln_stats(X, bX, PRx, bPRx, PRy, bPRy, XBT, bXBT, XSQ, bXSQ, ntok):
            nblk = ntok // 512
            for i in range((nblk + 3) // 4):
                for j4 in range(min(4, nblk - 4 * i)):
                    blk = 4 * i + j4
                    for c in range(NCH):
                        V(lambda e, c=c, blk=blk: e.tensor_copy(XBT[:, 0:512],
                                                                X[c][:, 512 * blk:512 * (blk + 1)]),
                          r=[bX[c], bXBT], w=[bXBT])
                        V(lambda e: e.tensor_tensor(XSQ[:, 0:512], XBT[:, 0:512], XBT[:, 0:512],
                                                    ALU.mult), r=[bXBT], w=[bXSQ])
                        TT(lambda e, c=c, j4=j4: e.matmul(PRx[32 * j4:32 * j4 + 1, 0:512],
                                                          ONESC_B[:], XBT[:, 0:512],
                                                          start=(c == 0), stop=(c == NCH - 1),
                                                          tile_position=(0, 32 * j4)),
                           r=[bONESC_B, bXBT], w=[bPRx])
                        TT(lambda e, c=c, j4=j4: e.matmul(PRy[32 * j4:32 * j4 + 1, 0:512],
                                                          ONESC_H[:], XSQ[:, 0:512],
                                                          start=(c == 0), stop=(c == NCH - 1),
                                                          tile_position=(0, 32 * j4)),
                           r=[bONESC_H, bXSQ], w=[bPRy])
                V(lambda e, i=i: e.tensor_scalar(RMU[i][0][0:97, :], PRx[0:97, 0:512],
                                                 1.0 / D, None, ALU.mult), r=[bPRx], w=[RMU[i][1]])
                V(lambda e, i=i: e.tensor_scalar(RVR[i][0][0:97, :], PRy[0:97, 0:512],
                                                 1.0 / D, None, ALU.mult), r=[bPRy], w=[RVR[i][1]])
            nh = (nblk + 3) // 4
            for i in range(nh):
                mv, mb = sview(RMU, i)
                vv, vb = sview(RVR, i)
                sv_, sb_ = sview(RSC, i)
                rv, rb = sview(RRH, i)
                V(lambda e, sv_=sv_, mv=mv: e.tensor_tensor(sv_, mv, mv, ALU.mult), r=[mb], w=[sb_])
                V(lambda e, vv=vv, sv_=sv_: e.tensor_tensor(vv, vv, sv_, ALU.subtract),
                  r=[vb, sb_], w=[vb])
                V(lambda e, rv=rv, vv=vv: e.tensor_scalar(rv, vv, float(LN_EPS), None, ALU.add),
                  r=[vb], w=[rb])
                SS(lambda e, rv=rv: e.activation(rv, rv, AF.Ln), r=[rb], w=[rb])
                SS(lambda e, rv=rv: e.activation(rv, rv, AF.Exp, scale=-0.5), r=[rb], w=[rb])

        def emit_attn_a(X, bX):
            with contextlib.ExitStack() as pp:
                PS1, bPS1 = pst(pp, "PS1a", [128, 1024])
                PS2, bPS2 = pst(pp, "PS2a", [128, 1024])
                PC, bPC = pst(pp, "PCa", [65, 2048])
                XBT, bXBT = sbt(pp, "XBTa", [128, 512], BF16)
                XSQ, bXSQ = sbt(pp, "XSQa", [128, 512], F16)
                MUB, bMUB = sbt(pp, "MUBa", [128, 2048], BF16)
                RHOB, bRHOB = sbt(pp, "RHOBa", [128, 2048], BF16)
                SLICE, bSLICE = sbt(pp, "SLICEa", [128, 2048], BF16)
                QOP, bQOP = sbt(pp, "QOPa", [128, 2048], BF16)
                KOP, bKOP = sbt(pp, "KOPa", [128, 2048], BF16)
                AT0, bAT0 = sbt(pp, "AT0a", [128, 1024], BF16)
                AT1, bAT1 = sbt(pp, "AT1a", [128, 1024], BF16)
                CB, bCB = sbt(pp, "CBa", [64, 2048], BF16)
                RQK, bRQK = sbt(pp, "RQKa", [64, 2048], BF16)
                RSUM, bRSUM = sbt(pp, "RSUMa", [64, 1024], F32)
                XUPD, bXUPD = sbt(pp, "XUPDa", [128, 2048], F32)
                SC0, bSC0 = sbt(pp, "SC0a", [128, 512], F32)
                VFM, bVFM = CB, bCB
                print("attn_a: sbuf left:", nc.sbuf_bytes_remaining)

                ln_stats(X, bX, PS1, bPS1, PS2, bPS2, XBT, bXBT, XSQ, bXSQ, NTOK)

                def fm64(lhs_t, lhs_buf, wcols, dst_ap, dst_buf, r0, PSx, bPSx):
                    for sg in range(4):
                        o = 512 * (sg % 2)
                        TT(lambda e, sg=sg, o=o, r0=r0: e.matmul(
                            PSx[0:64, o:o + 512], lhs_t[r0:r0 + 64, wcols[0]:wcols[1]],
                            SLICE[r0:r0 + 64, 512 * sg:512 * (sg + 1)],
                            start=True, stop=True),
                           r=[lhs_buf, bSLICE], w=[bPSx])
                        V(lambda e, sg=sg, o=o, dst_ap=dst_ap: e.tensor_copy(
                            dst_ap[:, 512 * sg:512 * (sg + 1)], PSx[0:64, o:o + 512]),
                          r=[bPSx], w=[dst_buf])

                def fm64_cb(lhs_ap, lhs_buf, dst_lo, dst_ap, dst_buf, PSx, bPSx, mul_rqk=False):
                    for sg in range(4):
                        o = 512 * (sg % 2)
                        TT(lambda e, sg=sg, o=o, lhs_ap=lhs_ap: e.matmul(
                            PSx[0:64, o:o + 512], lhs_ap, CB[:, 512 * sg:512 * (sg + 1)],
                            start=True, stop=True),
                           r=[lhs_buf, bCB], w=[bPSx])
                        if mul_rqk:
                            V(lambda e, sg=sg, o=o: e.tensor_tensor(
                                dst_ap[dst_lo:dst_lo + 64, 512 * sg:512 * (sg + 1)],
                                PSx[0:64, o:o + 512], RQK[:, 512 * sg:512 * (sg + 1)], ALU.mult),
                              r=[bPSx, bRQK], w=[dst_buf])
                        else:
                            V(lambda e, sg=sg, o=o: e.tensor_copy(
                                dst_ap[dst_lo:dst_lo + 64, 512 * sg:512 * (sg + 1)],
                                PSx[0:64, o:o + 512]),
                              r=[bPSx], w=[dst_buf])

                def rms_row(scale):
                    for sg in range(4):
                        V(lambda e, sg=sg: e.tensor_tensor(XSQ[0:64, 0:512],
                                                           CB[:, 512 * sg:512 * (sg + 1)],
                                                           CB[:, 512 * sg:512 * (sg + 1)], ALU.mult),
                          r=[bCB, bXSQ], w=[bXSQ])
                        TT(lambda e, sg=sg: e.matmul(PS1[32 * sg:32 * sg + 1, 0:512],
                                                     ONESC_H[0:64, :], XSQ[0:64, 0:512],
                                                     start=True, stop=True,
                                                     tile_position=(0, 32 * sg)),
                           r=[bONESC_H, bXSQ], w=[bPS1])
                    V(lambda e: e.tensor_scalar(RSC[0][0][0:97, :], PS1[0:97, 0:512],
                                                1.0 / 64.0, float(RMS_EPS), ALU.mult, ALU.add),
                      r=[bPS1], w=[RSC[0][1]])
                    rv, rb = sview(RSC, 0)
                    SS(lambda e, rv=rv: e.activation(rv, rv, AF.Ln), r=[rb], w=[rb])
                    SS(lambda e, rv=rv: e.activation(rv, rv, AF.Exp, scale=-0.5), r=[rb], w=[rb])
                    replicate(RSC, 0, 4, RQK, bRQK, 64, PS1, bPS1, 1024, scale=scale)

                def colmax_sq(OP, bOP, cbase):
                    for sg in range(4):
                        o = 512 * (sg % 2)
                        V(lambda e, sg=sg, OP=OP: e.tensor_tensor(
                            XSQ[:, 0:512], OP[:, 512 * sg:512 * (sg + 1)],
                            OP[:, 512 * sg:512 * (sg + 1)], ALU.mult),
                          r=[bOP, bXSQ], w=[bXSQ])
                        TT(lambda e, o=o: e.matmul(PS1[0:1, o:o + 512], ONESC_H[:], XSQ[:, 0:512],
                                                   start=True, stop=True),
                           r=[bONESC_H, bXSQ], w=[bPS1])
                        V(lambda e, sg=sg, o=o, cbase=cbase: e.reduce_max(
                            SC11[0:1, cbase + sg:cbase + sg + 1], PS1[0:1, o:o + 512], axis=AX.X),
                          r=[bPS1], w=[bSC11])

                for b in range(NSEQ):
                    t0 = b * T
                    replicate(RMU, 4 * b, 4, MUB, bMUB, 128, PS1, bPS1, 1024)
                    replicate(RRH, 4 * b, 4, RHOB, bRHOB, 128, PS1, bPS1, 1024)
                    for c in range(NCH):
                        for sg in range(4):
                            V(lambda e, c=c, sg=sg, t0=t0: e.tensor_tensor(
                                SC0[:], X[c][:, t0 + 512 * sg:t0 + 512 * (sg + 1)],
                                MUB[:, 512 * sg:512 * (sg + 1)], ALU.subtract),
                              r=[bX[c], bMUB], w=[bSC0])
                            V(lambda e, sg=sg: e.tensor_tensor(
                                SLICE[:, 512 * sg:512 * (sg + 1)], SC0[:],
                                RHOB[:, 512 * sg:512 * (sg + 1)], ALU.mult),
                              r=[bSC0, bRHOB], w=[bSLICE])
                        for hh in range(2):
                            r0 = 64 * hh
                            fm64(WOVT, bWOVT, (0, 64), VFM, bVFM, r0, PS1, bPS1)
                            for tt_ in range(16):
                                QV(lambda e, tt_=tt_: e.dma_start_transpose(
                                    VTOK[:, 80 * tt_:80 * tt_ + 64], VFM[:, 128 * tt_:128 * (tt_ + 1)]),
                                  r=[bVFM, bVTOK], w=[bVTOK])
                            fm64(NTT, bNTT, (0, 64), CB, bCB, r0, PS2, bPS2)
                            rms_row(1.0)
                            V(lambda e: e.tensor_tensor(KOP[0:64, :], CB[:], RQK[:], ALU.mult),
                              r=[bCB, bRQK], w=[bKOP])
                            for sg in range(4):
                                o = 512 * (sg % 2)
                                TT(lambda e, sg=sg, o=o, r0=r0: e.matmul(
                                    PS2[0:64, o:o + 512], WKRTB[r0:r0 + 64, :],
                                    SLICE[r0:r0 + 64, 512 * sg:512 * (sg + 1)],
                                    start=True, stop=True),
                                   r=[bWKRTB, bSLICE], w=[bPS2])
                                V(lambda e, sg=sg, o=o: e.tensor_copy(
                                    KOP[64:128, 512 * sg:512 * (sg + 1)], PS2[0:64, o:o + 512]),
                                  r=[bPS2], w=[bKOP])
                            fm64(MQT, bMQT, (0, 64), CB, bCB, r0, PS2, bPS2)
                            rms_row(0.125)
                            V(lambda e: e.scalar_tensor_tensor(QOP[0:64, :], CB[:], WQK[:], RQK[:],
                                                               ALU.mult, ALU.mult),
                              r=[bCB, bWQK, bRQK], w=[bQOP])
                            fm64_cb(WQRP[:], bWQRP, 64, QOP, bQOP, PS2, bPS2, mul_rqk=True)
                            colmax_sq(QOP, bQOP, 0)
                            colmax_sq(KOP, bKOP, 4)
                            V(lambda e: e.reduce_max(SC11[0:1, 8:9], SC11[0:1, 0:4], axis=AX.X),
                              r=[bSC11], w=[bSC11])
                            V(lambda e: e.reduce_max(SC11[0:1, 9:10], SC11[0:1, 4:8], axis=AX.X),
                              r=[bSC11], w=[bSC11])
                            V(lambda e: e.tensor_tensor(SC11[0:1, 10:11], SC11[0:1, 8:9],
                                                        SC11[0:1, 9:10], ALU.mult),
                              r=[bSC11], w=[bSC11])
                            V(lambda e: e.tensor_scalar(SCW[0:1, :], ONESR_F[0:1, :],
                                                        SC11[0:1, 10:11], None, ALU.mult),
                              r=[bONESR_F, bSC11], w=[bSCW])
                            SS(lambda e: e.activation(SCW[0:1, :], SCW[0:1, :], AF.Ln),
                               r=[bSCW], w=[bSCW])
                            SS(lambda e: e.activation(SCW[0:1, :], SCW[0:1, :], AF.Exp,
                                                      scale=0.5), r=[bSCW], w=[bSCW])
                            TT(lambda e: e.matmul(PS1[:, 0:1], SCW[0:1, :], ONESR_F[0:1, 0:1],
                                                  start=True, stop=True),
                               r=[bONESR_F, bSCW], w=[bPS1])
                            V(lambda e: e.tensor_scalar(BIASV[:], PS1[:, 0:1], -1.0, None, ALU.mult),
                              r=[bPS1], w=[bBIASV])
                            if PARANOID >= 1:
                                s.barrier()
                            if DEBUG_H0 and b == 0 and c == 0 and hh == 0:
                                V(lambda e: e.tensor_copy(XUPD[:, 0:2048], QOP[:]), r=[bQOP], w=[bXUPD])
                                Q(lambda e: e.dma_start(out=dqop, in_=XUPD[:, 0:2048]), r=[bXUPD], w=[bXUPD])
                                V(lambda e: e.tensor_copy(XUPD[:, 0:2048], KOP[:]), r=[bKOP], w=[bXUPD])
                                Q(lambda e: e.dma_start(out=dkop, in_=XUPD[:, 0:2048]), r=[bXUPD], w=[bXUPD])
                                Q(lambda e: e.dma_start(out=dbias, in_=BIASV[:]), r=[bBIASV], w=[bBIASV])
                                Q(lambda e: e.dma_start(out=dsc, in_=SC11[:]), r=[bSC11], w=[bSC11])
                            for kb in range(16):
                                qlo = 128 * kb
                                qa0 = 512 * (qlo // 512)
                                qq = qa0
                                while qq < T:
                                    qhi = min(((qq // 1024) + 1) * 1024, T)
                                    PSs, bPSs, ATs, bATs = (PS1, bPS1, AT0, bAT0) if (qq // 1024) % 2 == 0 \
                                        else (PS2, bPS2, AT1, bAT1)
                                    o0 = qq % 1024
                                    for pz in range(qq, qhi, 512):
                                        TT(lambda e, kb=kb, pz=pz, PSs=PSs: e.matmul(
                                            PSs[:, pz % 1024:pz % 1024 + 512],
                                            KOP[:, 128 * kb:128 * (kb + 1)], QOP[:, pz:pz + 512],
                                            start=True, stop=True),
                                           r=[bKOP, bQOP], w=[bPSs])
                                    SS(lambda e, PSs=PSs, ATs=ATs, o0=o0, n=qhi - qq: e.activation(
                                        ATs[:, o0:o0 + n], PSs[:, o0:o0 + n], AF.Exp,
                                        bias=BIASV[:, 0:1], scale=1.0),
                                       r=[bPSs, bBIASV], w=[bATs])
                                    if qq == qa0:
                                        if qlo > qa0:
                                            V(lambda e, ATs=ATs, o0=o0, nz=qlo - qa0: e.memset(
                                                ATs[:, o0:o0 + nz], 0.0), r=[bATs], w=[bATs])
                                        V(lambda e, ATs=ATs, od=qlo % 1024: e.tensor_tensor(
                                            ATs[:, od:od + 128], ATs[:, od:od + 128], MASKB[:],
                                            ALU.mult), r=[bMASKB, bATs], w=[bATs])
                                    for pz in range(qq, qhi, 512):
                                        TT(lambda e, kb=kb, pz=pz, ATs=ATs: e.matmul(
                                            PC[0:65, pz:pz + 512], VTOK[:, 80 * kb:80 * kb + 65],
                                            ATs[:, pz % 1024:pz % 1024 + 512],
                                            start=(kb == 0), stop=(kb == min(15, (pz + 511) // 128))),
                                           r=[bVTOK, bATs], w=[bPC])
                                    qq = qhi
                                if PARANOID >= 2:
                                    s.barrier()
                                if DEBUG_H0 and b == 0 and c == 0 and hh == 0 and kb == 0:
                                    V(lambda e: e.tensor_copy(XUPD[:, 0:1024], AT0[:]), r=[bAT0], w=[bXUPD])
                                    Q(lambda e: e.dma_start(out=dat, in_=XUPD[:, 0:1024]), r=[bXUPD], w=[bXUPD])
                            if PARANOID >= 1:
                                s.barrier()
                            for sg in range(4):
                                V(lambda e, sg=sg: e.reciprocal(srow(RSC, sg),
                                                                PC[64:65, 512 * sg:512 * (sg + 1)]),
                                  r=[bPC], w=[RSC[0][1]])
                            for sg in range(2):
                                replicate(RSC, 2 * sg, 2, RSUM, bRSUM, 64, PS1, bPS1, 1024)
                                V(lambda e, sg=sg, r0=r0: e.tensor_tensor(
                                    XUPD[r0:r0 + 64, 1024 * sg:1024 * (sg + 1)],
                                    PC[0:64, 1024 * sg:1024 * (sg + 1)],
                                    RSUM[:, 0:1024], ALU.mult),
                                  r=[bPC, bRSUM], w=[bXUPD])
                        for sg in range(4):
                            V(lambda e, c=c, sg=sg, t0=t0: e.tensor_tensor(
                                X[c][:, t0 + 512 * sg:t0 + 512 * (sg + 1)],
                                X[c][:, t0 + 512 * sg:t0 + 512 * (sg + 1)],
                                XUPD[:, 512 * sg:512 * (sg + 1)], ALU.add),
                              r=[bXUPD, bX[c]], w=[bX[c]])
            s.barrier()

        def emit_ffn(X, bX, cc_idx0):
            sfx = str(cc_idx0)
            with contextlib.ExitStack() as pp:
                PM = [pst(pp, f"PMf{i}" + sfx, [128, 512]) for i in range(8)]
                PRa, bPRa = PM[4]
                PRb, bPRb = PM[5]
                XBT, bXBT = sbt(pp, "XBTf" + sfx, [128, 512], BF16)
                XSQ, bXSQ = sbt(pp, "XSQf" + sfx, [128, 512], F16)
                MUR, bMUR = sbt(pp, "MURf" + sfx, [128, 512], F32)
                RPR, bRPR = sbt(pp, "RPRf" + sfx, [128, 512], F32)
                SC0, bSC0 = sbt(pp, "SC0f" + sfx, [128, 512], F32)
                SC1, bSC1 = sbt(pp, "SC1f" + sfx, [128, 512], F32)
                BIGB, bBIGB = sbt(pp, "BIGBf" + sfx, [128, 8192], BF16)
                W1SL0, bW1SL0 = sbt(pp, "W1SL0" + sfx, [128, 1024], BF16)
                W1SL1, bW1SL1 = sbt(pp, "W1SL1" + sfx, [128, 1024], BF16)
                W2SL0, bW2SL0 = sbt(pp, "W2SL0" + sfx, [128, 1024], BF16)
                W2SL1, bW2SL1 = sbt(pp, "W2SL1" + sfx, [128, 1024], BF16)
                HEV = [sbt(pp, f"HEVf{i}" + sfx, [128, 512], F16) for i in range(4)]
                RLD0, bRLD0 = sbt(pp, "RLD0" + sfx, [128, 512], F16)
                RLD1, bRLD1 = sbt(pp, "RLD1" + sfx, [128, 512], F16)
                RHQ = [sbt(pp, f"RHQf{i}" + sfx, [128, 512], BF16) for i in range(2)]
                QU8, bQU8 = sbt(pp, "QU8f" + sfx, [128, 512], U8)
                HMX, bHMX = sbt(pp, "HMXf" + sfx, [128, 200], F32, small=True)
                bHSP = [[Buf() for _ in range(32)] for _ in range(8)]
                MAXC, bMAXC = sbt(pp, "MAXCf" + sfx, [128, 1], F32)
                print("ffn: sbuf left:", nc.sbuf_bytes_remaining)

                ln_stats(X, bX, PRa, bPRa, PRb, bPRb, XBT, bXBT, XSQ, bXSQ, NTOK)
                for blk in range(8):
                    replicate(RMU, blk, 1, MUR, bMUR, 128, PRa, bPRa, 512)
                    replicate(RRH, blk, 1, RPR, bRPR, 128, PRa, bPRa, 512)
                    for c in range(NCH):
                        V(lambda e, c=c, blk=blk: e.tensor_tensor(
                            SC0[:], X[c][:, 512 * blk:512 * (blk + 1)], MUR[:], ALU.subtract),
                          r=[bX[c], bMUR], w=[bSC0])
                        V(lambda e: e.tensor_tensor(SC1[:], SC0[:], RPR[:], ALU.mult),
                          r=[bSC0, bRPR], w=[bSC1])
                        V(lambda e, c=c, blk=blk: e.reduce_max(HMX[:, 8 * c + blk:8 * c + blk + 1],
                                                               SC1[:], axis=AX.X),
                          r=[bSC1], w=[bHMX])
                V(lambda e: e.reduce_max(MAXC[:], HMX[:, 0:64], axis=AX.X), r=[bHMX], w=[bMAXC])
                cross_part_max(MAXC[:], bMAXC, PRa, bPRa, 0)
                s.op("sync", lambda e, k=cc_idx0: e.dma_start(out=ccin[k], in_=SC11[0:1, 0:1]),
                     reads=[bSC11], writes=[bSC11], dma=True, chan="qc")
                G(lambda e, k=cc_idx0: e.collective_compute(
                    "AllReduce", ALU.max, replica_groups=[[0, 1, 2, 3, 4, 5, 6, 7]],
                    ins=[ccin[k]], outs=[ccout[k]]), r=[bSC11], w=[bSC11])
                s.op("sync", lambda e, k=cc_idx0: e.dma_start(out=SC11[0:1, 0:1], in_=ccout[k]),
                     reads=[bSC11], writes=[bSC11], dma=True, chan="qc")
                V(lambda e: e.tensor_scalar(SC11[0:1, 1:2], SC11[0:1, 0:1], float(Q_EPS), 1.0 / 255.0,
                                            ALU.max, ALU.mult), r=[bSC11], w=[bSC11])
                V(lambda e: e.reciprocal(SC11[0:1, 2:3], SC11[0:1, 1:2]), r=[bSC11], w=[bSC11])
                V(lambda e: e.tensor_scalar(SCW[0:1, :], ONESR_F[0:1, :], SC11[0:1, 1:2], None,
                                            ALU.mult), r=[bONESR_F, bSC11], w=[bSCW])
                TT(lambda e: e.matmul(PRa[:, 0:1], SCW[0:1, :], ONESR_F[0:1, 0:1],
                                      start=True, stop=True), r=[bONESR_F, bSCW], w=[bPRa])
                V(lambda e: e.tensor_copy(S1V[:, 0:1], PRa[:, 0:1]), r=[bPRa], w=[bS1V])
                V(lambda e: e.tensor_scalar(SCW[0:1, :], ONESR_F[0:1, :], SC11[0:1, 2:3], None,
                                            ALU.mult), r=[bONESR_F, bSC11], w=[bSCW])
                TT(lambda e: e.matmul(PRa[:, 0:1], SCW[0:1, :], ONESR_F[0:1, 0:1],
                                      start=True, stop=True), r=[bONESR_F, bSCW], w=[bPRa])
                V(lambda e: e.tensor_copy(S1V[:, 1:2], PRa[:, 0:1]), r=[bPRa], w=[bS1V])
                for i in range(2):
                    r2v, r2b = sview(RR2, i)
                    rhv, rhb = sview(RRH, i)
                    V(lambda e, r2v=r2v, rhv=rhv: e.tensor_scalar(r2v, rhv, S1V[0:97, 1:2], None,
                                                                  ALU.mult),
                      r=[rhb, bS1V], w=[r2b])
                V(lambda e: e.tensor_scalar(SV1[:], SW1[:], S1V[:, 0:1], None, ALU.mult),
                  r=[bSW1, bS1V], w=[bSV1])
                V(lambda e: e.reciprocal(SC0[:, 0:32], SV1[:]), r=[bSV1], w=[bSC0])
                V(lambda e: e.tensor_tensor(SC1[:, 0:32], B1S[:], SC0[:, 0:32], ALU.mult),
                  r=[bB1S, bSC0], w=[bSC1])
                V(lambda e: e.tensor_copy(SCI1[:], SC1[:, 0:32]), r=[bSC1], w=[bSCI1])
                V(lambda e: e.tensor_tensor(BQ1[:], SCI1[:], SV1[:], ALU.mult),
                  r=[bSCI1, bSV1], w=[bBQ1])
                for qt in range(4):
                    tq0 = 1024 * qt
                    for blk in range(2):
                        gblk = 2 * qt + blk
                        replicate(RMU, gblk, 1, MUR, bMUR, 128, PRa, bPRa, 512)
                        replicate(RR2, gblk, 1, RPR, bRPR, 128, PRa, bPRa, 512)
                        for c in range(NCH):
                            V(lambda e, c=c, gblk=gblk: e.tensor_tensor(
                                SC0[:], X[c][:, 512 * gblk:512 * (gblk + 1)], MUR[:], ALU.subtract),
                              r=[bX[c], bMUR], w=[bSC0])
                            V(lambda e: e.tensor_tensor(QU8[:], SC0[:], RPR[:], ALU.mult),
                              r=[bSC0, bRPR], w=[bQU8])
                            V(lambda e, c=c, blk=blk: e.tensor_copy(
                                BIGB[:, 1024 * c + 512 * blk:1024 * c + 512 * (blk + 1)], QU8[:]),
                              r=[bQU8], w=[bBIGB])
                    for fc in range(32):
                        WS, bWS = (W1SL0, bW1SL0) if fc % 2 == 0 else (W1SL1, bW1SL1)
                        QL(lambda e, fc=fc, WS=WS: e.dma_start(out=WS[:], in_=wint1t[fc]),
                          r=[bWS], w=[bWS])
                        for blk in range(2):
                            pm, bpm = PM[(fc * 2 + blk) % 4]
                            for dc in range(NCH):
                                TT(lambda e, blk=blk, dc=dc, pm=pm, WS=WS: e.matmul(
                                    pm[:, 0:512], WS[:, 128 * dc:128 * (dc + 1)],
                                    BIGB[:, 1024 * dc + 512 * blk:1024 * dc + 512 * (blk + 1)],
                                    start=(dc == 0), stop=(dc == 7)),
                                   r=[bWS, bBIGB], w=[bpm])
                            hv, bhv = HEV[(fc * 2 + blk) % 4]
                            tb = 2 * qt + blk
                            V(lambda e, fc=fc, pm=pm: e.tensor_scalar(
                                SC1[:], pm[:, 0:512], SV1[:, fc:fc + 1], BQ1[:, fc:fc + 1],
                                ALU.mult, ALU.add), r=[bpm, bSV1, bBQ1], w=[bSC1])
                            midx = fc * 2 + blk
                            V(lambda e, hv=hv, midx=midx: e.tensor_scalar(
                                hv[:], SC1[:], 0.0, -1e30, ALU.max, ALU.max,
                                accum_out=HMX[:, 64 + midx:65 + midx]),
                              r=[bSC1], w=[bhv, bHMX])
                            s.op("scalar", lambda e, tb=tb, fc=fc, hv=hv: e.dma_start(
                                out=hsp[tb, fc], in_=hv[:]),
                                 reads=[bhv], writes=[bhv, bHSP[tb][fc]], dma=True, chan="qs")
                    V(lambda e, qt=qt: e.reduce_max(HMX[:, 192 + qt:193 + qt], HMX[:, 64:128],
                                                    axis=AX.X), r=[bHMX], w=[bHMX])
                    s.barrier()
                V(lambda e: e.reduce_max(MAXC[:], HMX[:, 192:196], axis=AX.X), r=[bHMX], w=[bMAXC])
                cross_part_max(MAXC[:], bMAXC, PRa, bPRa, 0)
                s.op("sync", lambda e, k=cc_idx0 + 1: e.dma_start(out=ccin[k], in_=SC11[0:1, 0:1]),
                     reads=[bSC11], writes=[bSC11], dma=True, chan="qc")
                G(lambda e, k=cc_idx0 + 1: e.collective_compute(
                    "AllReduce", ALU.max, replica_groups=[[0, 1, 2, 3, 4, 5, 6, 7]],
                    ins=[ccin[k]], outs=[ccout[k]]), r=[bSC11], w=[bSC11])
                s.op("sync", lambda e, k=cc_idx0 + 1: e.dma_start(out=SC11[0:1, 0:1], in_=ccout[k]),
                     reads=[bSC11], writes=[bSC11], dma=True, chan="qc")
                V(lambda e: e.tensor_scalar(SC11[0:1, 1:2], SC11[0:1, 0:1], float(Q_EPS), 1.0 / 255.0,
                                            ALU.max, ALU.mult), r=[bSC11], w=[bSC11])
                V(lambda e: e.reciprocal(SC11[0:1, 2:3], SC11[0:1, 1:2]), r=[bSC11], w=[bSC11])
                V(lambda e: e.tensor_scalar(SCW[0:1, :], ONESR_F[0:1, :], SC11[0:1, 1:2], None,
                                            ALU.mult), r=[bONESR_F, bSC11], w=[bSCW])
                TT(lambda e: e.matmul(PRa[:, 0:1], SCW[0:1, :], ONESR_F[0:1, 0:1],
                                      start=True, stop=True), r=[bONESR_F, bSCW], w=[bPRa])
                V(lambda e: e.tensor_copy(S1V[:, 0:1], PRa[:, 0:1]), r=[bPRa], w=[bS1V])
                V(lambda e: e.tensor_scalar(SCW[0:1, :], ONESR_F[0:1, :], SC11[0:1, 2:3], None,
                                            ALU.mult), r=[bONESR_F, bSC11], w=[bSCW])
                TT(lambda e: e.matmul(PRa[:, 0:1], SCW[0:1, :], ONESR_F[0:1, 0:1],
                                      start=True, stop=True), r=[bONESR_F, bSCW], w=[bPRa])
                V(lambda e: e.tensor_copy(S1V[:, 1:2], PRa[:, 0:1]), r=[bPRa], w=[bS1V])
                V(lambda e: e.tensor_scalar(SV2[:], SW2[:], S1V[:, 0:1], None, ALU.mult),
                  r=[bSW2, bS1V], w=[bSV2])
                V(lambda e: e.reciprocal(SC0[:, 0:8], SV2[:]), r=[bSV2], w=[bSC0])
                V(lambda e: e.tensor_tensor(SC1[:, 0:8], B2S[:], SC0[:, 0:8], ALU.mult),
                  r=[bB2S, bSC0], w=[bSC1])
                V(lambda e: e.tensor_copy(SCI1[:, 0:8], SC1[:, 0:8]), r=[bSC1], w=[bSCI1])
                V(lambda e: e.tensor_tensor(BQ2[:], SCI1[:, 0:8], SV2[:], ALU.mult),
                  r=[bSCI1, bSV2], w=[bBQ2])
                s.barrier()
                if DEBUG_H0 and cc_idx0 == 0:
                    for tb_ in range(8):
                        for fc in range(32):
                            Q(lambda e, tb_=tb_, fc=fc: e.dma_start(out=dhsp[tb_, fc], in_=hsp[tb_, fc]),
                              r=[bHSP[tb_][fc]], w=[bHSP[tb_][fc]])
                for tb in range(8):
                    s.barrier()
                    for fc in range(32):
                        RLD, bRLD = (RLD0, bRLD0) if fc % 2 == 0 else (RLD1, bRLD1)
                        rhq, brhq = RHQ[fc % 2]
                        WS, bWS = (W2SL0, bW2SL0) if fc % 2 == 0 else (W2SL1, bW2SL1)
                        s.op("sync", lambda e, tb=tb, fc=fc, RLD=RLD: e.dma_start(
                            out=RLD[:], in_=hsp[tb, fc]),
                             reads=[bRLD, bHSP[tb][fc]], writes=[bRLD], dma=True, chan="qr")
                        QL(lambda e, fc=fc, WS=WS: e.dma_start(out=WS[:], in_=wint2t[fc]),
                          r=[bWS], w=[bWS])
                        V(lambda e, RLD=RLD: e.tensor_scalar(QU8[:], RLD[:], S1V[:, 1:2], None,
                                                             ALU.mult),
                          r=[bRLD, bS1V], w=[bQU8])
                        V(lambda e, rhq=rhq: e.tensor_copy(rhq[:], QU8[:]), r=[bQU8], w=[brhq])
                        if DEBUG_H0 and cc_idx0 == 0:
                            Q(lambda e, tb=tb, fc=fc, rhq=rhq: e.dma_start(out=drh[tb, fc], in_=rhq[:]),
                              r=[brhq], w=[brhq])
                        for dc in range(NCH):
                            pm, bpm = PM[dc]
                            TT(lambda e, fc=fc, dc=dc, pm=pm, WS=WS, rhq=rhq: e.matmul(
                                pm[:, 0:512], WS[:, 128 * dc:128 * (dc + 1)], rhq[:],
                                start=(fc == 0), stop=(fc == 31)),
                               r=[bWS, brhq], w=[bpm])
                    for dc in range(NCH):
                        pm, bpm = PM[dc]
                        V(lambda e, dc=dc, pm=pm: e.tensor_scalar(
                            SC1[:], pm[:, 0:512], SV2[:, dc:dc + 1], BQ2[:, dc:dc + 1],
                            ALU.mult, ALU.add), r=[bpm, bSV2, bBQ2], w=[bSC1])
                        if DEBUG_H0 and cc_idx0 == 0:
                            Q(lambda e, tb=tb, dc=dc: e.dma_start(out=dbr[tb, dc], in_=SC1[:]),
                              r=[bSC1], w=[bSC1])
                        V(lambda e, dc=dc, tb=tb: e.tensor_tensor(
                            X[dc][:, 512 * tb:512 * (tb + 1)], X[dc][:, 512 * tb:512 * (tb + 1)],
                            SC1[:], ALU.add), r=[bSC1, bX[dc]], w=[bX[dc]])
            s.barrier()

        def emit_attn_b():
            with contextlib.ExitStack() as pp:
                PS1, bPS1 = pst(pp, "PS1b", [128, 1024])
                PS2, bPS2 = pst(pp, "PS2b", [128, 1024])
                PC, bPC = pst(pp, "PCb", [65, 2048])
                XS = []
                bXS = []
                for c in range(NCH):
                    tt_, bb_ = sbt(pp, f"XSb{c}", [128, T], F32)
                    XS.append(tt_)
                    bXS.append(bb_)
                XBT, bXBT = sbt(pp, "XBTb", [128, 512], BF16)
                XSQ, bXSQ = sbt(pp, "XSQb", [128, 512], F16)
                ZNB, bZNB = sbt(pp, "ZNBb", [128, 16384], BF16)
                CTXB, bCTXB = sbt(pp, "CTXBb", [128, 16384], BF16)
                WPS0, bWPS0 = sbt(pp, "WPS0b", [128, 1024], BF16)
                WPS1, bWPS1 = sbt(pp, "WPS1b", [128, 1024], BF16)
                QKVW0, bQKVW0 = sbt(pp, "QKVW0b", [128, 1024], BF16)
                QKVW1, bQKVW1 = sbt(pp, "QKVW1b", [128, 1024], BF16)
                MU32, bMU32 = sbt(pp, "MU32b", [128, 512], F32)
                R132, bR132 = sbt(pp, "R132b", [128, 512], F32)
                MUB, bMUB = sbt(pp, "MUBb", [128, 2048], BF16)
                RHOB, bRHOB = sbt(pp, "RHOBb", [128, 2048], BF16)
                QA, bQA, KA, bKA = [], [], [], []
                for i in range(2):
                    tq, bq_ = sbt(pp, f"QAb{i}", [65, 2048], BF16)
                    QA.append(tq)
                    bQA.append(bq_)
                    tk, bk_ = sbt(pp, f"KAb{i}", [65, 2048], BF16)
                    KA.append(tk)
                    bKA.append(bk_)
                AT0, bAT0 = sbt(pp, "AT0b", [128, 1024], BF16)
                AT1, bAT1 = sbt(pp, "AT1b", [128, 1024], BF16)
                RSUM, bRSUM = sbt(pp, "RSUMb", [64, 1024], F32)
                SC0, bSC0 = sbt(pp, "SC0b", [128, 512], F32)
                SC1, bSC1 = sbt(pp, "SC1b", [128, 512], F32)
                print("attn_b: sbuf left:", nc.sbuf_bytes_remaining)

                V(lambda e: e.memset(KA[0][64:65, :], 1.0), r=[bKA[0]], w=[bKA[0]])
                V(lambda e: e.memset(KA[1][64:65, :], 1.0), r=[bKA[1]], w=[bKA[1]])

                for b in range(NSEQ):
                    t0 = b * T
                    for c in range(NCH):
                        Q(lambda e, c=c, t0=t0: e.dma_start(out=XS[c][:], in_=xsp[c][:, t0:t0 + T]),
                          r=[bXS[c]], w=[bXS[c]])
                    ln_stats(XS, bXS, PS1, bPS1, PS2, bPS2, XBT, bXBT, XSQ, bXSQ, T)
                    scv, scb = sview(RSC, 0)
                    r1v, r1b = sview(RRH, 0)
                    vrv, vrb = sview(RVR, 0)
                    r2v, r2b = sview(RR2, 0)
                    V(lambda e: e.tensor_tensor(scv, r1v, r1v, ALU.mult), r=[r1b], w=[scb])
                    V(lambda e: e.tensor_tensor(scv, scv, vrv, ALU.mult), r=[scb, vrb], w=[scb])
                    V(lambda e: e.tensor_scalar(scv, scv, float(LN_EPS), None, ALU.add),
                      r=[scb], w=[scb])
                    SS(lambda e: e.activation(scv, scv, AF.Ln), r=[scb], w=[scb])
                    SS(lambda e: e.activation(scv, scv, AF.Exp, scale=-0.5), r=[scb], w=[scb])
                    V(lambda e: e.tensor_tensor(r2v, r1v, scv, ALU.mult), r=[r1b, scb], w=[r2b])
                    replicate(RMU, 0, 4, MUB, bMUB, 128, PS1, bPS1, 1024)
                    replicate(RR2, 0, 4, RHOB, bRHOB, 128, PS1, bPS1, 1024)
                    for c in range(NCH):
                        for sg in range(4):
                            V(lambda e, c=c, sg=sg: e.tensor_tensor(
                                SC0[:], XS[c][:, 512 * sg:512 * (sg + 1)],
                                MUB[:, 512 * sg:512 * (sg + 1)], ALU.subtract),
                              r=[bXS[c], bMUB], w=[bSC0])
                            V(lambda e, sg=sg: e.tensor_tensor(
                                SC1[:], SC0[:], RHOB[:, 512 * sg:512 * (sg + 1)], ALU.mult),
                              r=[bSC0, bRHOB], w=[bSC1])
                            V(lambda e, c=c, sg=sg: e.tensor_scalar(
                                ZNB[:, 2048 * c + 512 * sg:2048 * c + 512 * (sg + 1)], SC1[:],
                                LNS[:, c:c + 1], LNB[:, c:c + 1], ALU.mult, ALU.add),
                              r=[bSC1, bLNS, bLNB], w=[bZNB])
                    for p in range(8):
                        for g in range(3):
                            QW, bQW = (QKVW0, bQKVW0) if g % 2 == 0 else (QKVW1, bQKVW1)
                            for dc in range(NCH):
                                QL(lambda e, g=g, p=p, dc=dc, QW=QW: e.dma_start(
                                    out=QW[:, 128 * dc:128 * (dc + 1)],
                                    in_=qkvb[g, p, 128 * dc:128 * (dc + 1), :]),
                                  r=[bQW], w=[bQW])
                            for sg in range(4):
                                PSs, bPSs = (PS1, bPS1) if sg % 2 == 0 else (PS2, bPS2)
                                for dc in range(NCH):
                                    TT(lambda e, sg=sg, dc=dc, PSs=PSs, QW=QW: e.matmul(
                                        PSs[:, 0:512],
                                        QW[:, 128 * dc:128 * (dc + 1)],
                                        ZNB[:, 2048 * dc + 512 * sg:2048 * dc + 512 * (sg + 1)],
                                        start=(dc == 0), stop=(dc == 7)),
                                       r=[bQW, bZNB], w=[bPSs])
                                if g == 0:
                                    for hh in range(2):
                                        V(lambda e, sg=sg, hh=hh, PSs=PSs: e.tensor_copy(
                                            QA[hh][0:64, 512 * sg:512 * (sg + 1)],
                                            PSs[64 * hh:64 * hh + 64, 0:512]),
                                          r=[bPSs], w=[bQA[hh]])
                                elif g == 1:
                                    for hh in range(2):
                                        V(lambda e, sg=sg, hh=hh, PSs=PSs: e.tensor_copy(
                                            KA[hh][0:64, 512 * sg:512 * (sg + 1)],
                                            PSs[64 * hh:64 * hh + 64, 0:512]),
                                          r=[bPSs], w=[bKA[hh]])
                                else:
                                    V(lambda e, p=p, sg=sg, PSs=PSs: e.tensor_copy(
                                        CTXB[:, 2048 * p + 512 * sg:2048 * p + 512 * (sg + 1)],
                                        PSs[:, 0:512]),
                                      r=[bPSs], w=[bCTXB])
                        for hh in range(2):
                            hr0 = 64 * hh
                            for tt_ in range(16):
                                QV(lambda e, p=p, tt_=tt_, hr0=hr0: e.dma_start_transpose(
                                    VTOK[:, 80 * tt_:80 * tt_ + 64],
                                    CTXB[hr0:hr0 + 64, 2048 * p + 128 * tt_:2048 * p + 128 * (tt_ + 1)]),
                                  r=[bCTXB, bVTOK], w=[bVTOK])
                            for sg in range(4):
                                o = 512 * (sg % 2)
                                V(lambda e, sg=sg, hh=hh: e.tensor_tensor(
                                    XSQ[0:64, 0:512], KA[hh][0:64, 512 * sg:512 * (sg + 1)],
                                    KA[hh][0:64, 512 * sg:512 * (sg + 1)], ALU.mult),
                                  r=[bKA[hh], bXSQ], w=[bXSQ])
                                TT(lambda e, o=o: e.matmul(PS1[0:1, o:o + 512], ONESC_H[0:64, :],
                                                           XSQ[0:64, 0:512], start=True, stop=True),
                                   r=[bONESC_H, bXSQ], w=[bPS1])
                                V(lambda e, sg=sg, o=o: e.reduce_max(
                                    SC11[0:1, sg:sg + 1], PS1[0:1, o:o + 512], axis=AX.X),
                                  r=[bPS1], w=[bSC11])
                            V(lambda e: e.reduce_max(SC11[0:1, 4:5], SC11[0:1, 0:4], axis=AX.X),
                              r=[bSC11], w=[bSC11])
                            SS(lambda e: e.activation(SC11[0:1, 4:5], SC11[0:1, 4:5], AF.Ln),
                               r=[bSC11], w=[bSC11])
                            SS(lambda e: e.activation(SC11[0:1, 4:5], SC11[0:1, 4:5], AF.Exp,
                                                      scale=0.5), r=[bSC11], w=[bSC11])
                            TT(lambda e: e.matmul(PS1[:, 0:1], ONESR_F[0:1, :], SC11[0:1, 4:5],
                                                  start=True, stop=True),
                               r=[bONESR_F, bSC11], w=[bPS1])
                            V(lambda e: e.tensor_copy(BIASV[:], PS1[:, 0:1]), r=[bPS1], w=[bBIASV])
                            for sg in range(4):
                                V(lambda e, sg=sg, hh=hh: e.tensor_tensor(
                                    XSQ[0:64, 0:512], QA[hh][0:64, 512 * sg:512 * (sg + 1)],
                                    QA[hh][0:64, 512 * sg:512 * (sg + 1)], ALU.mult),
                                  r=[bQA[hh], bXSQ], w=[bXSQ])
                                TT(lambda e, sg=sg: e.matmul(PS1[32 * sg:32 * sg + 1, 0:512],
                                                             ONESC_H[0:64, :], XSQ[0:64, 0:512],
                                                             start=True, stop=True,
                                                             tile_position=(0, 32 * sg)),
                                   r=[bONESC_H, bXSQ], w=[bPS1])
                            V(lambda e: e.tensor_copy(RSC[0][0][0:97, :], PS1[0:97, 0:512]),
                              r=[bPS1], w=[RSC[0][1]])
                            scv2, scb2 = sview(RSC, 0)
                            SS(lambda e: e.activation(scv2, scv2, AF.Ln), r=[scb2], w=[scb2])
                            SS(lambda e: e.activation(scv2, scv2, AF.Exp, scale=0.5),
                               r=[scb2], w=[scb2])
                            for sg in range(4):
                                V(lambda e, sg=sg, hh=hh: e.tensor_scalar(
                                    QA[hh][64:65, 512 * sg:512 * (sg + 1)], srow(RSC, sg),
                                    BIASV[32 * sg:32 * sg + 1, 0:1], -1.0, ALU.mult, ALU.mult),
                                  r=[RSC[0][1], bBIASV], w=[bQA[hh]])
                            for qh in range(2):
                                qo = 1024 * qh
                                for kb in range(16):
                                    PSs, bPSs, ATs, bATs = (PS1, bPS1, AT0, bAT0) if kb % 2 == 0 \
                                        else (PS2, bPS2, AT1, bAT1)
                                    for sg in range(2):
                                        TT(lambda e, kb=kb, qo=qo, sg=sg, hh=hh, PSs=PSs: e.matmul(
                                            PSs[:, 512 * sg:512 * (sg + 1)],
                                            KA[hh][0:65, 128 * kb:128 * (kb + 1)],
                                            QA[hh][0:65, qo + 512 * sg:qo + 512 * (sg + 1)],
                                            start=True, stop=True),
                                           r=[bKA[hh], bQA[hh]], w=[bPSs])
                                    SS(lambda e, PSs=PSs, ATs=ATs: e.activation(ATs[:], PSs[:], AF.Exp),
                                       r=[bPSs], w=[bATs])
                                    for sg in range(2):
                                        TT(lambda e, kb=kb, qo=qo, sg=sg, ATs=ATs: e.matmul(
                                            PC[0:65, qo + 512 * sg:qo + 512 * (sg + 1)],
                                            VTOK[:, 80 * kb:80 * kb + 65],
                                            ATs[:, 512 * sg:512 * (sg + 1)],
                                            start=(kb == 0), stop=(kb == 15)),
                                           r=[bVTOK, bATs], w=[bPC])
                            for sg in range(4):
                                V(lambda e, sg=sg: e.reciprocal(srow(RSC, sg),
                                                                PC[64:65, 512 * sg:512 * (sg + 1)]),
                                  r=[bPC], w=[RSC[0][1]])
                            for sg in range(2):
                                replicate(RSC, 2 * sg, 2, RSUM, bRSUM, 64, PS1, bPS1, 1024)
                                V(lambda e, p=p, sg=sg, hr0=hr0: e.tensor_tensor(
                                    CTXB[hr0:hr0 + 64, 2048 * p + 1024 * sg:2048 * p + 1024 * (sg + 1)],
                                    PC[0:64, 1024 * sg:1024 * (sg + 1)],
                                    RSUM[:, 0:1024], ALU.mult),
                                  r=[bPC, bRSUM], w=[bCTXB])
                    for sg in range(4):
                        replicate(RMU, sg, 1, MU32, bMU32, 128, PS1, bPS1, 512)
                        replicate(RRH, sg, 1, R132, bR132, 128, PS1, bPS1, 512)
                        for oc in range(NCH):
                            WP, bWP = (WPS0, bWPS0) if oc % 2 == 0 else (WPS1, bWPS1)
                            QL(lambda e, oc=oc, WP=WP: e.dma_start(out=WP[:], in_=wprojb[oc]),
                              r=[bWP], w=[bWP])
                            pm, bpm = (PS1, bPS1) if oc % 2 == 0 else (PS2, bPS2)
                            for dc in range(NCH):
                                TT(lambda e, sg=sg, dc=dc, pm=pm, WP=WP: e.matmul(
                                    pm[:, 0:512],
                                    WP[:, 128 * dc:128 * (dc + 1)],
                                    CTXB[:, 2048 * dc + 512 * sg:2048 * dc + 512 * (sg + 1)],
                                    start=(dc == 0), stop=(dc == 7)),
                                   r=[bWP, bCTXB], w=[bpm])
                            V(lambda e, oc=oc, sg=sg: e.tensor_tensor(
                                SC0[:], XS[oc][:, 512 * sg:512 * (sg + 1)],
                                MU32[:], ALU.subtract),
                              r=[bXS[oc], bMU32], w=[bSC0])
                            V(lambda e: e.tensor_tensor(
                                SC0[:], SC0[:], R132[:], ALU.mult),
                              r=[bSC0, bR132], w=[bSC0])
                            V(lambda e, oc=oc, pm=pm: e.scalar_tensor_tensor(
                                SC1[:], pm[:, 0:512], BPJ[:, oc:oc + 1], SC0[:], ALU.add, ALU.add),
                              r=[bpm, bBPJ, bSC0], w=[bSC1])
                            V(lambda e, oc=oc, sg=sg: e.tensor_tensor(
                                XS[oc][:, 512 * sg:512 * (sg + 1)], XS[oc][:, 512 * sg:512 * (sg + 1)],
                                SC1[:], ALU.add), r=[bSC1, bXS[oc]], w=[bXS[oc]])
                    for c in range(NCH):
                        Q(lambda e, c=c, t0=t0: e.dma_start(out=xsp[c][:, t0:t0 + T], in_=XS[c][:]),
                          r=[bXS[c]], w=[bXS[c]])
            s.barrier()

        def store_y(X, bX):
            for c in range(NCH):
                for b in range(NSEQ):
                    Q(lambda e, c=c, b=b: e.dma_start(out=y[b, 128 * c:128 * (c + 1), :],
                                                      in_=X[c][:, b * T:(b + 1) * T]),
                      r=[bX[c]], w=[bX[c]])

        with contextlib.ExitStack() as pxa:
            X, bX = [], []
            for c in range(NCH):
                tt_, bb_ = sbt(pxa, f"X{c}", [128, NTOK], F32)
                X.append(tt_)
                bX.append(bb_)
            for c in range(NCH):
                for b in range(NSEQ):
                    Q(lambda e, c=c, b=b: e.dma_start(out=X[c][:, b * T:(b + 1) * T],
                                                      in_=x_in[b, 128 * c:128 * (c + 1), :]),
                      r=[bX[c]], w=[bX[c]])
            if n_sub >= 1:
                emit_attn_a(X, bX)
            if n_sub >= 2:
                emit_ffn(X, bX, 0)
            if n_sub <= 2:
                store_y(X, bX)
            else:
                for c in range(NCH):
                    Q(lambda e, c=c: e.dma_start(out=xsp[c], in_=X[c][:]), r=[bX[c]], w=[bX[c]])
            s.barrier()

        if n_sub >= 3:
            emit_attn_b()
            with contextlib.ExitStack() as pxb:
                X2, bX2 = [], []
                for c in range(NCH):
                    tt_, bb_ = sbt(pxb, f"X2{c}", [128, NTOK], F32)
                    X2.append(tt_)
                    bX2.append(bb_)
                for c in range(NCH):
                    Q(lambda e, c=c: e.dma_start(out=X2[c][:], in_=xsp[c]), r=[bX2[c]], w=[bX2[c]])
                if n_sub >= 4:
                    emit_ffn(X2, bX2, 2)
                store_y(X2, bX2)
                s.barrier()

        s.barrier()
        with nc.Block() as block:
            s.finalize(block)
        tot = sum(len(v) for v in s.steps.values())
        print("total steps:", tot, {e: len(v) for e, v in s.steps.items()})
    return nc


_cache = {}


def kernel(**ins):
    f32 = np.float32
    x = np.ascontiguousarray(np.asarray(ins["L0_attn_a_x"], f32).transpose(0, 2, 1))
    wq = np.asarray(ins["L0_attn_b_w_qkv"], f32)
    wqkv_p = np.ascontiguousarray(wq.reshape(D, H, 64, 3).transpose(0, 3, 1, 2).reshape(D, 3 * D))
    common = {
        "ident": np.eye(128, dtype=f32),
        "wdq": np.ascontiguousarray(np.asarray(ins["L0_attn_a_W_DQ"], f32)),
        "wuqt": np.ascontiguousarray(np.asarray(ins["L0_attn_a_W_UQ"], f32).T),
        "wdkv": np.ascontiguousarray(np.asarray(ins["L0_attn_a_W_DKV"], f32)),
        "wukvt": np.ascontiguousarray(np.asarray(ins["L0_attn_a_W_UKV"], f32).T),
        "wqrt": np.ascontiguousarray(np.asarray(ins["L0_attn_a_W_QR"], f32).T),
        "wkrt": np.ascontiguousarray(np.asarray(ins["L0_attn_a_W_KR"], f32).T),
        "wot": np.ascontiguousarray(np.asarray(ins["L0_attn_a_W_O"], f32).T),
        "qnw": np.ascontiguousarray(np.asarray(ins["L0_attn_a_q_norm_w"], f32).reshape(64, 1)),
        "knw": np.ascontiguousarray(np.asarray(ins["L0_attn_a_k_norm_w"], f32).reshape(64, 1)),
        "lnsc": np.ascontiguousarray(np.asarray(ins["L0_attn_b_ln_scale"], f32).reshape(8, 128).T),
        "lnbi": np.ascontiguousarray(np.asarray(ins["L0_attn_b_ln_bias"], f32).reshape(8, 128).T),
        "bprojc": np.ascontiguousarray(np.asarray(ins["L0_attn_b_b_proj"], f32).reshape(8, 128).T),
        "wqkv": wqkv_p,
        "wproj": np.ascontiguousarray(np.asarray(ins["L0_attn_b_w_proj"], f32)),
        "w1": np.ascontiguousarray(np.asarray(ins["L0_ffn_w1"], f32)),
        "w1t": np.ascontiguousarray(np.asarray(ins["L0_ffn_w1"], f32).T),
        "b1c": np.ascontiguousarray(np.asarray(ins["L0_ffn_b1"], f32).reshape(32, 128).T),
        "w2": np.ascontiguousarray(np.asarray(ins["L0_ffn_w2"], f32)),
        "w2t": np.ascontiguousarray(np.asarray(ins["L0_ffn_w2"], f32).T),
        "b2c": np.ascontiguousarray(np.asarray(ins["L0_ffn_b2"], f32).reshape(8, 128).T),
    }
    key = (N_SUB, PARANOID, DEBUG_H0)
    if key not in _cache:
        _cache[key] = build_nc(N_SUB)
    nc = _cache[key]
    in_maps = []
    for i in range(8):
        m = dict(common)
        m["x_in"] = np.ascontiguousarray(x[2 * i:2 * i + 2])
        in_maps.append(m)
    res = run_bass_kernel_spmd(nc, in_maps, core_ids=list(range(8)))
    out = np.concatenate([np.asarray(r["y"]) for r in res.results], axis=0)
    return np.ascontiguousarray(out.transpose(0, 2, 1))
